# version 07
# speedup vs baseline: 1.6220x; 1.0121x over previous
import numpy as np
import concourse.bass as bass
import concourse.mybir as mybir
from concourse.bass_utils import run_bass_kernel_spmd

F32 = mybir.dt.float32
I32 = mybir.dt.int32
BF16 = mybir.dt.bfloat16
AF = mybir.ActivationFunctionType
OP = mybir.AluOpType
AX = mybir.AxisListType.X

D = 1024
T = 4096
FF = 4096
DT = D // 128
FT = FF // 128
TCH = 512
NCHUNK = T // TCH
NCORES = 8
HEADS = 8
DH = 128
CHUNK = 64
NCK = T // CHUNK
NE = 8

ENGINES = ("sync", "gpsimd", "scalar", "vector", "tensor")


class K:

    def __init__(self, nc):
        self.nc = nc
        self.ops = {e: [] for e in ENGINES}
        self.cnt = {e: 0 for e in ENGINES}
        self.sems = {e: nc.alloc_semaphore(name="S_" + e) for e in ENGINES}
        self.bufs = {}
        self.tok_map = {}

    def op(self, eng, fn, waits=(), inc=0):
        self.ops[eng].append((fn, self._flat(waits), inc))
        last = (eng, self.cnt[eng])
        for one in (inc if isinstance(inc, list) else [inc]):
            if isinstance(one, tuple):
                key = one[1]
                if key not in self.sems:
                    self.sems[key] = self.nc.alloc_semaphore(name=key)
                    self.cnt[key] = 0
                self.cnt[key] += one[2]
                last = (key, self.cnt[key])
            elif one:
                self.cnt[eng] += one
                last = (eng, self.cnt[eng])
        return last

    def tok(self, eng):
        return (eng, self.cnt[eng])

    def dtok(self, stream):
        key = "D_" + stream
        return (key, self.cnt.get(key, 0))

    def dma(self, eng, stream, out, in_, waits=(), **kw):
        key = "D_" + stream
        if key not in self.sems:
            self.sems[key] = self.nc.alloc_semaphore(name=key)
            self.cnt[key] = 0
            self.ops[key] = None
        self.ops[eng].append(
            (lambda E: E.dma_start(out, in_, **kw),
             [w for w in self._flat(waits)], ("sem", key, 16)))
        self.cnt[key] += 16
        return (key, self.cnt[key])

    def _flat(self, waits):
        out = []
        for w in waits:
            if w is None:
                continue
            if isinstance(w, list):
                out.extend(x for x in w if x is not None)
            else:
                out.append(w)
        return out

    def buf(self, name, shape, dtype):
        if name not in self.bufs:
            if getattr(self, "arena_base", None) is not None:
                per_part = 1
                for s in shape[1:]:
                    per_part *= s
                per_part *= mybir.dt.size(dtype)
                off = (self.arena_off + 31) & ~31
                self.arena_off = off + per_part
                assert self.arena_off <= self.arena_bytes, (
                    name, self.arena_off, self.arena_bytes)
                self.bufs[name] = self.nc.alloc_sbuf_tensor_at(
                    name, shape, dtype, offset=self.arena_base + off)
            else:
                self.bufs[name] = self.nc.alloc_sbuf_tensor(name, shape, dtype)
        return self.bufs[name]

    def arena_init(self, nbytes):
        host = self.nc.alloc_sbuf_tensor("ARENA", [128, nbytes], mybir.dt.uint8)
        self.arena_host = host
        self.arena_base = self.nc.lookup_mloc(host).addr
        self.arena_bytes = nbytes
        self.arena_off = 0

    def arena_reset(self):
        self.arena_off = 0

    def full_sync(self):
        toks = []
        for e in ENGINES:
            toks.append(self.op(e, lambda E: E.nop(), inc=1))
        dtoks = [(s, self.cnt[s]) for s in self.sems if s.startswith("D_")]
        waits = toks + dtoks
        for e in ENGINES:
            self.op(e, lambda E: E.nop(), waits=waits)

    def overlay(self, name, host, shape, dtype, byte_off=0):
        if name not in self.bufs:
            mloc = self.nc.lookup_mloc(host)
            self.bufs[name] = self.nc.alloc_sbuf_tensor_at(
                name, shape, dtype, offset=mloc.addr + byte_off)
        return self.bufs[name]

    def get(self, key):
        return self.tok_map.get(key)

    def set(self, key, tok):
        self.tok_map[key] = tok

    def emit(self):
        nc = self.nc
        last_wait = {e: {} for e in ENGINES}
        with nc.Block() as block:
            for eng in ENGINES:
                def make(eng):
                    def body(E):
                        lw = last_wait[eng]
                        for fn, waits, inc in self.ops[eng]:
                            byps = {}
                            for (pe, v) in waits:
                                byps[pe] = max(v, byps.get(pe, 0))
                            for pe, v in byps.items():
                                if v > lw.get(pe, 0):
                                    E.wait_ge(self.sems[pe], v)
                                    lw[pe] = v
                            ins = fn(E)
                            for one in (inc if isinstance(inc, list) else [inc]):
                                if isinstance(one, tuple):
                                    ins.then_inc(self.sems[one[1]], one[2])
                                elif one:
                                    ins.then_inc(self.sems[eng], one)
                    return body
                getattr(block, eng)(make(eng))


def make_identity_select(nc, ident):
    sq = ident.shape[-1]
    return nc.gpsimd.affine_select(
        out=ident[:, :], in_=ident[:, :],
        compare_op=OP.not_equal, fill=1.0, base=0,
        pattern=[[-1, sq]], channel_multiplier=1)


def col_view(vec_dram, j):
    return vec_dram[j * 128:(j + 1) * 128].rearrange("(p o) -> p o", o=1)



def emit_identities(k):
    nc = k.nc
    ident_bf = k.buf("ident_bf", [128, 128], BF16)
    ident_f = k.buf("ident_f", [128, 128], F32)
    tm1 = k.op("gpsimd", lambda E: E.memset(ident_bf[:, :], 0.0), inc=1)
    tm2 = k.op("gpsimd", lambda E: E.memset(ident_f[:, :], 0.0), inc=1)
    t1 = k.op("gpsimd", lambda E: make_identity_select(nc, ident_bf),
              waits=[tm1], inc=1)
    t2 = k.op("gpsimd", lambda E: make_identity_select(nc, ident_f),
              waits=[tm2], inc=1)
    k.set("ident", t2)
    kcon = k.buf("kcon", [128, 4], F32)
    k.op("gpsimd", lambda E: E.memset(kcon[:, 0:1], 1e-5), inc=0)
    k.op("gpsimd", lambda E: E.memset(kcon[:, 1:2], 1e-6), inc=0)
    k.op("gpsimd", lambda E: E.memset(kcon[:, 2:3], 0.0), inc=0)
    tc = k.op("gpsimd", lambda E: E.memset(kcon[:, 3:4], 1.0), inc=1)
    k.set("kcon", tc)
    return ident_bf, ident_f


def transpose_via(k, psum_ap, src_ap, ident, waits, slot_key):
    return k.op("tensor", lambda E: E.transpose(psum_ap, src_ap, ident),
                waits=list(waits) + [k.get(slot_key), k.get("ident")], inc=1)



def build_l0(k, io, stages=3):
    nc = k.nc
    ident_bf = k.bufs["ident_bf"]
    ident_f = k.bufs["ident_f"]
    pb = k.pb
    pbt = k.pbt
    pbb = k.pbb

    wabcT = k.buf("wabcT", [128, 3 * 64 * 128], BF16)
    xnt = k.buf("xnt", [128, DT * TCH], BF16)
    aT = k.buf("aT", [128, DT * TCH], F32)
    uT = k.buf("uT", [128, DT * TCH], F32)
    cT = k.buf("cT", [128, DT * TCH], F32)
    scT = k.buf("scT", [128, DT * TCH], F32)
    a1m = k.buf("a1m", [128, 2 * TCH], F32)
    bT1 = k.buf("bT1", [128, 2 * TCH], F32)
    carry = k.buf("carry", [128, DT], F32)
    h_sb = k.buf("h_sb", [128, FT * TCH], BF16)
    w1t = k.buf("w1t", [128, 2 * DT * 128], BF16)
    w2t = k.buf("w2t", [128, 2 * FT * 128], BF16)
    gT1 = k.buf("gT1", [128, 2 * TCH], F32)
    mT1 = k.buf("mT1", [128, 2 * TCH], F32)
    xres = k.buf("xres", [128, 4 * D], F32)

    wtile_f = k.overlay("wtile_f", cT, [128, 2 * D], F32)
    wtile_b = k.overlay("wtile_b", scT, [128, 2 * D], BF16)
    wrow_f = k.overlay("wrow_f", h_sb, [128, 2 * FF], F32)
    sg = k.overlay("sg", aT, [128, FF], F32)
    strip = k.overlay("strip", uT, [128, 2 * FF], BF16)
    xtile = k.overlay("xtile", scT, [128, 2 * D], F32)
    xn_f = k.overlay("xn_f", cT, [128, 2 * D], F32)
    xn_b = k.overlay("xn_b", gT1, [128, 2 * D], BF16)
    xnstage = k.overlay("xnstage", mT1, [128, 4 * 128], BF16)

    if stages == 0:
        return [k.get("kcon"), k.get("ident")]


    def wabcT_ap(wi, kt, et):
        o = ((wi * DT + et) * DT + kt) * 128
        return wabcT[:, o:o + 128]
    for wi, wname in enumerate(("Wa", "Wb", "Wc")):
        wsrc = io[wname]
        for et in range(DT):
            sl = et % 2
            tload = k.dma("sync", "wab%d" % sl, wtile_f[:, sl * D:(sl + 1) * D],
                          wsrc[et * 128:(et + 1) * 128, :],
                          waits=[k.get(("wtl", sl))])
            tcast = k.op("vector", lambda E, sl=sl: E.tensor_copy(
                wtile_b[:, sl * D:(sl + 1) * D],
                wtile_f[:, sl * D:(sl + 1) * D]),
                waits=[tload, k.get(("wtb", sl))], inc=1)
            k.set(("wtl", sl), tcast)
            last = None
            for kt in range(DT):
                ps = pbb[kt % 2][:, :]
                tp = transpose_via(
                    k, ps, wtile_b[:, sl * D + kt * 128: sl * D + (kt + 1) * 128],
                    ident_bf[:, :], [tcast], ("pbf", kt % 2))
                tev = k.op("vector", lambda E, kt=kt, wi=wi, et=et, ps=ps: E.tensor_copy(
                    wabcT_ap(wi, kt, et), ps), waits=[tp], inc=1)
                k.set(("pbf", kt % 2), tev)
                last = tev
            k.set(("wtb", sl), last)
    wabc_done = k.tok("vector")
    if stages == 1:
        return [wabc_done]

    alpha = k.buf("alpha", [128, 1], F32)
    wrow_b = k.overlay("wrow_b", xres, [128, FF], BF16)
    for (wname, rows, cols, dstname) in (("w1", FF, D, "bw1T"), ("w2", D, FF, "bw2T")):
        wsrc = io[wname]
        dst = io[dstname]
        nblk = cols // 128
        for rt in range(rows // 128):
            sl = rt % 2
            tload = k.dma("sync", "wrow%d" % sl, wrow_f[:, sl * FF: sl * FF + cols],
                          wsrc[rt * 128:(rt + 1) * 128, :],
                          waits=[k.get(("wrl", sl))])
            tred = k.op("vector", lambda E, sl=sl, cols=cols: E.tensor_reduce(
                alpha[:, :], wrow_f[:, sl * FF: sl * FF + cols], AX, OP.add,
                apply_absolute_value=True),
                waits=[tload, k.get("alpha")], inc=1)
            tsg = k.op("scalar", lambda E, sl=sl, cols=cols: E.activation(
                sg[:, :cols], wrow_f[:, sl * FF: sl * FF + cols], AF.Sign),
                waits=[tload, k.get("sg")], inc=1)
            tbw = k.op("vector", lambda E, cols=cols: E.tensor_scalar(
                wrow_b[:, :cols], sg[:, :cols],
                alpha[:, :], 1.0 / cols, OP.mult, OP.mult),
                waits=[tsg, tred, k.get("wrow_b")], inc=1)
            k.set(("wrl", sl), [tsg, tbw])
            k.set("alpha", tbw)
            k.set("sg", tbw)
            ssl = rt % 2
            evs = []
            for ct in range(nblk):
                ps = pbb[ct % 2][:, :]
                tp = transpose_via(k, ps, wrow_b[:, ct * 128:(ct + 1) * 128],
                                   ident_bf[:, :], [tbw], ("pbf", ct % 2))
                tev = k.op("vector", lambda E, ct=ct, ssl=ssl, ps=ps: E.tensor_copy(
                    strip[:, ssl * FF + ct * 128: ssl * FF + (ct + 1) * 128], ps),
                    waits=[tp, k.get(("strip", ssl))], inc=1)
                k.set(("pbf", ct % 2), tev)
                evs.append(tev)
            k.set("wrow_b", evs[-1])
            if wname == "w1":
                tst = k.dma("gpsimd", "stbw%d" % ssl, dst[rt, :, :],
                            strip[:, ssl * FF: ssl * FF + nblk * 128], waits=evs)
            else:
                tst = k.dma("gpsimd", "stbw%d" % ssl, dst[rt, :, :],
                            strip[:, ssl * FF: ssl * FF + nblk * 128], waits=evs)
            k.set(("strip", ssl), tst)
    wprep_done = [k.dtok("stbw0"), k.dtok("stbw1")]

    babc = k.buf("babc", [128, 3 * DT], F32)
    for i, bn in enumerate(("ba", "bb", "bc")):
        for j in range(DT):
            k.dma("sync", "bias", babc[:, i * DT + j:i * DT + j + 1], col_view(io[bn], j))
    b1c = k.buf("b1c", [128, FT], F32)
    for j in range(FT):
        k.dma("sync", "bias", b1c[:, j:j + 1], col_view(io["b1"], j))
    b2c = k.buf("b2c", [128, DT], F32)
    for j in range(DT):
        k.dma("sync", "bias", b2c[:, j:j + 1], col_view(io["b2"], j))
    bias_ld = k.dtok("bias")
    nba = k.buf("nba", [128, DT], F32)
    bias_done = k.op("vector", lambda E: E.tensor_scalar(
        nba[:, :], babc[:, 0:DT], -1.0, None, OP.mult),
        waits=[bias_ld], inc=1)

    if stages < 2:
        return wprep_done + [bias_done]

    xncol = k.buf("xncol", [128, 8], F32)
    p0_bundle = [k.tok("tensor"), k.tok("vector"), k.tok("scalar"), k.tok("gpsimd")]
    for sl in range(2):
        k.set(("xt", sl), p0_bundle)
        k.set(("xnf", sl), p0_bundle)
    for tt in range(T // 128):
        sl = tt % 2
        tload = k.dma("sync", "x%d" % sl, xtile[:, sl * D:(sl + 1) * D],
                      io["x"][tt * 128:(tt + 1) * 128, :],
                      waits=[k.get(("xt", sl))])
        tsum = k.op("vector", lambda E, sl=sl: E.tensor_reduce(
            xncol[:, 0:1], xtile[:, sl * D:(sl + 1) * D], AX, OP.add),
            waits=[tload, k.get("xncol")], inc=1)
        tsq = k.op("scalar", lambda E, sl=sl: E.activation(
            xn_f[:, sl * D:(sl + 1) * D], xtile[:, sl * D:(sl + 1) * D],
            AF.Square, accum_out=xncol[:, 1:2]),
            waits=[tload, tsum, k.get(("xnf", sl))], inc=1)
        tmv = k.op("vector", lambda E: E.tensor_scalar(
            xncol[:, 2:4], xncol[:, 0:2], 1.0 / D, None, OP.mult),
            waits=[tsq, tsum], inc=1)
        tms = k.op("vector", lambda E: E.tensor_mul(
            xncol[:, 4:5], xncol[:, 2:3], xncol[:, 2:3]), waits=[tmv], inc=1)
        tvar = k.op("vector", lambda E: E.tensor_sub(
            xncol[:, 5:6], xncol[:, 3:4], xncol[:, 4:5]), waits=[tms], inc=1)
        tsd = k.op("scalar", lambda E: E.activation(
            xncol[:, 6:7], xncol[:, 5:6], AF.Sqrt,
            bias=k.bufs["kcon"][:, 0:1]),
            waits=[tvar, k.get("kcon")], inc=1)
        trs = k.op("vector", lambda E: E.reciprocal(
            xncol[:, 7:8], xncol[:, 6:7]), waits=[tsd], inc=1)
        txn = k.op("vector", lambda E, sl=sl: E.tensor_scalar(
            xn_f[:, sl * D:(sl + 1) * D], xtile[:, sl * D:(sl + 1) * D],
            xncol[:, 2:3], xncol[:, 7:8], OP.subtract, OP.mult),
            waits=[trs], inc=1)
        k.set(("xt", sl), txn)
        k.set("xncol", txn)
        tcb = k.op("vector", lambda E, sl=sl: E.tensor_copy(
            xn_b[:, sl * D:(sl + 1) * D], xn_f[:, sl * D:(sl + 1) * D]),
            waits=[txn], inc=1)
        k.set(("xnf", sl), tcb)
        for dt_ in range(DT):
            ps = pbb[dt_ % 2][:, :]
            tp = transpose_via(k, ps,
                               xn_b[:, sl * D + dt_ * 128: sl * D + (dt_ + 1) * 128],
                               ident_bf[:, :], [tcb], ("pbf", dt_ % 2))
            tev = k.op("vector", lambda E, dt_=dt_, ps=ps: E.tensor_copy(
                xnstage[:, (dt_ % 4) * 128:(dt_ % 4 + 1) * 128], ps),
                waits=[tp, k.get(("xnstg", dt_ % 4))], inc=1)
            k.set(("pbf", dt_ % 2), tev)
            tst = k.dma("gpsimd", "stxn%d" % (dt_ % 4), io["xnT"][dt_, :, tt * 128:(tt + 1) * 128],
                        xnstage[:, (dt_ % 4) * 128:(dt_ % 4 + 1) * 128],
                        waits=[tev])
            k.set(("xnstg", dt_ % 4), tst)
    p1_done = [k.dtok("stxn%d" % i) for i in range(4)]

    if stages < 3:
        return p1_done

    prep_bundle = ([k.tok("tensor"), k.tok("vector"), k.tok("scalar"),
                    k.tok("gpsimd"), k.tok("sync")]
                   + wprep_done + p1_done)
    for et in range(DT):
        k.set(("aT", et), prep_bundle)
        k.set(("uT", et), prep_bundle)
        k.set(("cT", et), prep_bundle)
    for ht in range(FT):
        k.set(("hsb", ht), prep_bundle)
    for sl in range(2):
        k.set(("pbank", sl), prep_bundle)
        k.set(("pbank", 2 + sl), prep_bundle)
    k.set("xres_free", prep_bundle)
    for sl in range(2):
        k.set(("w1t", sl), wprep_done)
        k.set(("w2t", sl), wprep_done)

    cinit = k.op("gpsimd", lambda E: E.memset(carry[:, :], 0.0), inc=1)

    for c in range(NCHUNK):
        xnt_toks = []
        for dt_ in range(DT):
            xnt_toks.append(k.dma(
                "sync", "xnt", xnt[:, dt_ * TCH:(dt_ + 1) * TCH],
                io["xnT"][dt_, :, c * TCH:(c + 1) * TCH],
                waits=[k.get("xnt_free")] + (p1_done if c == 0 else [])))
        xres_toks = []
        for tt in range(4):
            xres_toks.append(k.dma(
                "gpsimd", "xres", xres[:, tt * D:(tt + 1) * D],
                io["x"][c * TCH + tt * 128: c * TCH + (tt + 1) * 128, :],
                waits=[k.get("xres_free")]))

        for et in range(DT):
            for wi in range(3):
                gidx = et * 3 + wi
                bank = pb[gidx % 2]
                for kt in range(DT):
                    k.op("tensor", lambda E, wi=wi, kt=kt, et=et, bank=bank: E.matmul(
                        bank[:, :], wabcT_ap(wi, kt, et),
                        xnt[:, kt * TCH:(kt + 1) * TCH],
                        start=(kt == 0), stop=(kt == DT - 1)),
                        waits=([wabc_done, bias_done, cinit,
                                k.get(("pbank", gidx % 2))] + xnt_toks)
                        if kt == 0 else [],
                        inc=1 if kt == DT - 1 else 0)
                mmt = k.tok("tensor")
                if wi == 0:
                    t1 = k.op("scalar", lambda E, et=et, bank=bank: E.activation(
                        aT[:, et * TCH:(et + 1) * TCH], bank[:, :],
                        AF.Sigmoid, bias=babc[:, et:et + 1]),
                        waits=[mmt, k.get(("aT", et))], inc=1)
                    t2 = k.op("scalar", lambda E, et=et, bank=bank: E.activation(
                        a1m[:, (et % 2) * TCH:(et % 2 + 1) * TCH], bank[:, :],
                        AF.Sigmoid, scale=-1.0, bias=nba[:, et:et + 1]),
                        waits=[k.get(("a1m", et % 2))], inc=1)
                    k.set(("pbank", gidx % 2), t2)
                elif wi == 1:
                    t1 = k.op("scalar", lambda E, et=et, bank=bank: E.activation(
                        bT1[:, (et % 2) * TCH:(et % 2 + 1) * TCH], bank[:, :],
                        AF.Tanh, bias=babc[:, DT + et:DT + et + 1]),
                        waits=[mmt, k.get(("bT1", et % 2))], inc=1)
                    tu = k.op("vector", lambda E, et=et: E.tensor_mul(
                        uT[:, et * TCH:(et + 1) * TCH],
                        a1m[:, (et % 2) * TCH:(et % 2 + 1) * TCH],
                        bT1[:, (et % 2) * TCH:(et % 2 + 1) * TCH]),
                        waits=[t1, k.get(("uT", et))], inc=1)
                    k.set(("a1m", et % 2), tu)
                    k.set(("bT1", et % 2), tu)
                    k.set(("pbank", gidx % 2), t1)
                else:
                    t1 = k.op("scalar", lambda E, et=et, bank=bank: E.activation(
                        cT[:, et * TCH:(et + 1) * TCH], bank[:, :],
                        AF.Sigmoid, bias=babc[:, 2 * DT + et:2 * DT + et + 1]),
                        waits=[mmt, k.get(("cT", et))], inc=1)
                    k.set(("pbank", gidx % 2), t1)
        abc_s = k.tok("scalar")

        for dt_ in range(DT):
            tsc = k.op("vector", lambda E, dt_=dt_: E.tensor_tensor_scan(
                scT[:, dt_ * TCH:(dt_ + 1) * TCH],
                aT[:, dt_ * TCH:(dt_ + 1) * TCH],
                uT[:, dt_ * TCH:(dt_ + 1) * TCH],
                carry[:, dt_:dt_ + 1], OP.mult, OP.add),
                waits=[abc_s], inc=1)
            k.op("vector", lambda E, dt_=dt_: E.tensor_copy(
                carry[:, dt_:dt_ + 1],
                scT[:, (dt_ + 1) * TCH - 1:(dt_ + 1) * TCH]),
                waits=[tsc], inc=1)
            tst = k.op("vector", lambda E, dt_=dt_: E.tensor_mul(
                uT[:, dt_ * TCH:(dt_ + 1) * TCH],
                cT[:, dt_ * TCH:(dt_ + 1) * TCH],
                scT[:, dt_ * TCH:(dt_ + 1) * TCH]), inc=1)
            k.set(("aT", dt_), tst)
            k.set(("cT", dt_), tst)
        sT_done = k.tok("vector")

        for ht in range(FT):
            sl = ht % 2
            wload = k.dma("sync", "w1t%d" % sl, w1t[:, sl * DT * 128:(sl + 1) * DT * 128],
                          io["bw1T"][ht, :, :],
                          waits=[k.get(("w1t", sl))])
            bank = pb[2 + sl]
            for kt in range(DT):
                k.op("tensor", lambda E, kt=kt, sl=sl, bank=bank: E.matmul(
                    bank[:, :],
                    w1t[:, (sl * DT + kt) * 128:(sl * DT + kt + 1) * 128],
                    xnt[:, kt * TCH:(kt + 1) * TCH],
                    start=(kt == 0), stop=(kt == DT - 1)),
                    waits=[wload, k.get(("pbank", 2 + sl))] if kt == 0 else [],
                    inc=1 if kt == DT - 1 else 0)
            mmt = k.tok("tensor")
            k.set(("w1t", sl), mmt)
            tg = k.op("scalar", lambda E, ht=ht, bank=bank: E.activation(
                h_sb[:, ht * TCH:(ht + 1) * TCH], bank[:, :], AF.Gelu,
                bias=b1c[:, ht:ht + 1]),
                waits=[mmt, k.get(("hsb", ht))], inc=1)
            k.set(("pbank", 2 + sl), tg)

        h_done_s = k.tok("scalar")
        for et in range(DT):
            sl = et % 2
            w2load = k.dma("sync", "w2t%d" % sl, w2t[:, sl * FT * 128:(sl + 1) * FT * 128],
                           io["bw2T"][et, :, :],
                           waits=[k.get(("w2t", sl))])
            bank = pb[sl]
            for k2 in range(FT):
                k.op("tensor", lambda E, k2=k2, sl=sl, bank=bank: E.matmul(
                    bank[:, :],
                    w2t[:, (sl * FT + k2) * 128:(sl * FT + k2 + 1) * 128],
                    h_sb[:, k2 * TCH:(k2 + 1) * TCH],
                    start=(k2 == 0), stop=(k2 == FT - 1)),
                    waits=[w2load, h_done_s, k.get(("pbank", sl))] if k2 == 0 else [],
                    inc=1 if k2 == FT - 1 else 0)
            mmt = k.tok("tensor")
            k.set(("w2t", sl), mmt)
            tg = k.op("scalar", lambda E, et=et, sl=sl, bank=bank: E.activation(
                gT1[:, sl * TCH:(sl + 1) * TCH], bank[:, :], AF.Silu,
                bias=b2c[:, et:et + 1]),
                waits=[mmt], inc=1)
            tm = k.op("vector", lambda E, et=et, sl=sl: E.tensor_mul(
                mT1[:, sl * TCH:(sl + 1) * TCH],
                uT[:, et * TCH:(et + 1) * TCH],
                gT1[:, sl * TCH:(sl + 1) * TCH]),
                waits=[tg, sT_done], inc=1)
            k.set(("pbank", sl), tm)
            k.set(("uT", et), tm)
            for tt in range(4):
                ps = pbt[tt % 2][:, :]
                tp = k.op("tensor", lambda E, tt=tt, sl=sl, ps=ps: E.transpose(
                    ps, mT1[:, sl * TCH + tt * 128: sl * TCH + (tt + 1) * 128],
                    ident_f[:, :]),
                    waits=[tm, k.get(("pbt", tt % 2))] +
                          (xres_toks if et == 0 else []),
                    inc=1)
                ta = k.op("vector", lambda E, tt=tt, et=et, ps=ps: E.tensor_add(
                    xres[:, tt * D + et * 128: tt * D + (et + 1) * 128],
                    xres[:, tt * D + et * 128: tt * D + (et + 1) * 128],
                    ps),
                    waits=[tp], inc=1)
                k.set(("pbt", tt % 2), ta)
        res_done = k.tok("vector")

        st_toks = []
        for tt in range(4):
            st_toks.append(k.dma(
                "gpsimd", "stx1", io["x1"][c * TCH + tt * 128: c * TCH + (tt + 1) * 128, :],
                xres[:, tt * D:(tt + 1) * D], waits=[res_done]))
        k.set("xres_free", st_toks[-1])
        k.set("xnt_free", k.tok("tensor"))
        for ht in range(FT):
            k.set(("hsb", ht), k.tok("tensor"))

    return k.dtok("stx1")



def _dram(nc, name, shape, dtype, kind):
    h = nc.dram_tensor(name, list(shape), dtype, kind=kind)
    return h


def build_l0_program(stages=3):
    nc = bass.Bass("TRN2", target_bir_lowering=False)
    io = {
        "x": _dram(nc, "x", (T, D), F32, "ExternalInput"),
        "Wa": _dram(nc, "Wa", (D, D), F32, "ExternalInput"),
        "ba": _dram(nc, "ba", (D,), F32, "ExternalInput"),
        "Wb": _dram(nc, "Wb", (D, D), F32, "ExternalInput"),
        "bb": _dram(nc, "bb", (D,), F32, "ExternalInput"),
        "Wc": _dram(nc, "Wc", (D, D), F32, "ExternalInput"),
        "bc": _dram(nc, "bc", (D,), F32, "ExternalInput"),
        "w1": _dram(nc, "w1", (FF, D), F32, "ExternalInput"),
        "b1": _dram(nc, "b1", (FF,), F32, "ExternalInput"),
        "w2": _dram(nc, "w2", (D, FF), F32, "ExternalInput"),
        "b2": _dram(nc, "b2", (D,), F32, "ExternalInput"),
        "xnT": _dram(nc, "xnT", (DT, 128, T), BF16, "Internal"),
        "bw1T": _dram(nc, "bw1T", (FT, 128, DT * 128), BF16, "Internal"),
        "bw2T": _dram(nc, "bw2T", (DT, 128, FT * 128), BF16, "Internal"),
        "x1": _dram(nc, "x1", (T, D), F32, "ExternalOutput"),
    }
    k = K(nc)
    k.pb = [nc.alloc_psum_tensor("pb%d" % i, [128, 512], F32) for i in range(4)]
    k.pbt = [nc.alloc_psum_tensor("pbt%d" % i, [128, 128], F32) for i in range(2)]
    k.pbb = [nc.alloc_psum_tensor("pbb%d" % i, [128, 128], BF16) for i in range(2)]
    emit_identities(k)
    last = build_l0(k, io, stages=stages)
    k.op("sync", lambda E: E.nop(), waits=last if isinstance(last, list) else [last])
    k.emit()
    return nc


def l0_in_map(inputs, x_core):
    m = {
        "x": x_core,
        "Wa": inputs["L0_attn_Wa"], "ba": inputs["L0_attn_ba"],
        "Wb": inputs["L0_attn_Wb"], "bb": inputs["L0_attn_bb"],
        "Wc": inputs["L0_attn_Wc"], "bc": inputs["L0_attn_bc"],
        "w1": inputs["L0_ffn_w1"], "b1": inputs["L0_ffn_b1"],
        "w2": inputs["L0_ffn_w2"], "b2": inputs["L0_ffn_b2"],
    }
    return {k_: np.ascontiguousarray(v) for k_, v in m.items()}


def kernel_unfused(**inputs):
    cores = list(range(NCORES))
    x = np.ascontiguousarray(np.asarray(inputs["L0_attn_inputs"]))
    nc0 = build_l0_program()
    res0 = run_bass_kernel_spmd(
        nc0, [l0_in_map(inputs, x[c]) for c in cores], core_ids=cores)
    x1 = [res0.results[c]["x1"] for c in cores]
    nc1 = build_l1a_program()
    res1 = run_bass_kernel_spmd(
        nc1, [l1a_in_map(inputs, x1[c]) for c in cores], core_ids=cores)
    nc2 = build_l1b_program()
    res2 = run_bass_kernel_spmd(
        nc2, [l1b_in_map(inputs, x1[c], res1.results[c]["xn1T"],
                         res1.results[c]["aff"], res1.results[c]["s1T"])
              for c in cores], core_ids=cores)
    return np.stack([res2.results[c]["out"] for c in cores], axis=0)



C0G = 2.0 * 0.01 / DH


def build_l1a(k, io):
    nc = k.nc
    ident_bf = k.bufs["ident_bf"]
    ident_f = k.bufs["ident_f"]
    pb = k.pb
    pbt = k.pbt
    pbb = k.pbb
    kcon = k.bufs["kcon"]

    wkv_sb = k.buf("wkv_sb", [128, 8 * 2048], BF16)
    wq_sb = k.buf("wq_sb", [128, 8 * 1024], BF16)
    s1s = k.buf("s1s", [128, 8192], F32)
    kT = k.buf("kT", [128, T], BF16)
    vT = k.buf("vT", [128, T], BF16)
    szb = k.buf("szb", [128, T], BF16)
    Gb = k.buf("Gb", [128, T], BF16)
    dzb = k.buf("dzb", [128, T], BF16)
    lrb = k.buf("lrb", [128, T], BF16)
    gm_t = k.buf("gm_t", [128, 8192], BF16)
    gd_t = k.buf("gd_t", [128, 8192], BF16)
    blk = k.buf("blk", [128, 8 * 128], BF16)
    sg_f = k.buf("sg_f", [128, 2 * TCH], F32)
    tmpf = k.buf("tmpf", [128, 2 * TCH], F32)
    wi_sl = k.buf("wi_sl", [128, 4 * 128], BF16)
    vals_b = k.buf("vals_b", [128, T], BF16)
    rowq = k.buf("rowq", [1, 3 * TCH], F32)

    s0 = k.overlay("s0", wkv_sb, [128, 8192], F32)
    msc = k.overlay("msc", kT, [128, 8192], F32)
    qTh = k.overlay("qTh", kT, [128, T], BF16)
    sqb = k.overlay("sqb", gm_t, [128, T], BF16)
    scale_bc = k.overlay("scale_bc", gd_t, [128, T], F32)
    x5 = k.overlay("x5", s1s, [128, 2 * D], F32)
    xn1f = k.overlay("xn1f", s1s, [128, D], F32, byte_off=8 * 2048)
    p5sq = k.overlay("p5sq", s1s, [128, D], F32, byte_off=8 * 2048 + 4096)
    xn1b = k.overlay("xn1b", s1s, [128, D], BF16, byte_off=8 * 2048 + 8192)
    r_b = k.overlay("r_b", s1s, [128, D], BF16, byte_off=8 * 2048 + 8192 + 2048)
    xtf = k.overlay("xtf", s1s, [128, 2 * 128], F32, byte_off=8 * 2048 + 8192 + 4096)
    p5st = k.overlay("p5st", s1s, [128, 4 * 128], BF16, byte_off=8 * 2048 + 8192 + 5120)
    scol = k.overlay("scol", s1s, [128, 8], F32, byte_off=8 * 2048 + 8192 + 6144)
    affsb = k.overlay("affsb", s1s, [128, 8], F32, byte_off=8 * 2048 + 8192 + 6208)
    rT_c = k.overlay("rT_c", s1s, [128, 8 * 512], BF16, byte_off=0)
    rTs_c = k.overlay("rTs_c", s1s, [128, 8 * 512], BF16, byte_off=8192)
    cm_sb = k.overlay("cm_sb", s1s, [128, 512], F32, byte_off=16384)
    a6st = k.overlay("a6st", s1s, [128, 2 * 512], BF16, byte_off=18432)
    md_sb = k.overlay("md_sb", s1s, [16, 64], F32, byte_off=20480)
    lrst = k.overlay("lrst", s1s, [8, 512], F32, byte_off=20736)
    cm_b = k.overlay("cm_b", s1s, [128, 512], BF16, byte_off=22784)
    wc_sb = k.overlay("wc_sb", wkv_sb, [128, 8 * 1024], BF16, byte_off=0)
    vals_c = k.overlay("vals_c", wq_sb, [128, 8 * 512], BF16, byte_off=0)
    c6st = k.overlay("c6st", wq_sb, [128, 2 * 512], F32, byte_off=8192)

    gcol = k.buf("gcol", [128, 2 * DT], F32)
    for j in range(DT):
        k.dma("sync", "bias", gcol[:, j:j + 1], col_view(io["g_s"], j))
        k.dma("sync", "bias", gcol[:, DT + j:DT + j + 1], col_view(io["g_r"], j))
    gld = k.dtok("bias")

    wrow = k.overlay("wrow", gd_t, [128, 2 * 2048], F32)
    for kt_ in range(DT):
        sl = kt_ % 2
        tl = k.dma("sync", "wkv%d" % sl, wrow[:, sl * 2048:(sl + 1) * 2048],
                   io["w_kv"][kt_ * 128:(kt_ + 1) * 128, :],
                   waits=[k.get(("wkvl", sl)), gld])
        tc = k.op("vector", lambda E, kt_=kt_, sl=sl: E.tensor_scalar_mul(
            wkv_sb[:, kt_ * 2048:(kt_ + 1) * 2048],
            wrow[:, sl * 2048:(sl + 1) * 2048],
            gcol[:, kt_:kt_ + 1]), waits=[tl], inc=1)
        k.set(("wkvl", sl), tc)
    for kt_ in range(DT):
        sl = kt_ % 2
        tl = k.dma("sync", "wkv%d" % sl, wrow[:, sl * 2048: sl * 2048 + 1024],
                   io["w_q"][kt_ * 128:(kt_ + 1) * 128, :],
                   waits=[k.get(("wkvl", sl))])
        tc = k.op("vector", lambda E, kt_=kt_, sl=sl: E.tensor_scalar_mul(
            wq_sb[:, kt_ * 1024:(kt_ + 1) * 1024],
            wrow[:, sl * 2048: sl * 2048 + 1024],
            gcol[:, DT + kt_:DT + kt_ + 1]), waits=[tl], inc=1)
        k.set(("wkvl", sl), tc)
    k.set("wsgl", [k.get(("wkvl", 0)), k.get(("wkvl", 1))])
    wsg_sb = k.buf("wsg_sb", [128, DT * 16], BF16)
    wmd_sb = k.buf("wmd_sb", [128, DT * 16], BF16)
    for kt_ in range(DT):
        tl1 = k.dma("sync", "wsg", wrow[:, 0:8],
                    io["w_step"][kt_ * 128:(kt_ + 1) * 128, :],
                    waits=[k.get("wsgl")])
        tl2 = k.dma("sync", "wsg", wrow[:, 8:16],
                    io["w_gate"][kt_ * 128:(kt_ + 1) * 128, :])
        tl3 = k.dma("sync", "wsg", wrow[:, 16:24],
                    io["w_mom"][kt_ * 128:(kt_ + 1) * 128, :])
        tl4 = k.dma("sync", "wsg", wrow[:, 24:32],
                    io["w_decay"][kt_ * 128:(kt_ + 1) * 128, :])
        t1 = k.op("vector", lambda E, kt_=kt_: E.tensor_scalar_mul(
            wsg_sb[:, kt_ * 16:kt_ * 16 + 8], wrow[:, 0:8],
            gcol[:, kt_:kt_ + 1]), waits=[tl1, tl2, tl3, tl4], inc=1)
        t2 = k.op("vector", lambda E, kt_=kt_: E.tensor_scalar_mul(
            wsg_sb[:, kt_ * 16 + 8:kt_ * 16 + 16], wrow[:, 8:16],
            gcol[:, DT + kt_:DT + kt_ + 1]), inc=1)
        t3 = k.op("vector", lambda E, kt_=kt_: E.tensor_scalar(
            wmd_sb[:, kt_ * 16:kt_ * 16 + 8], wrow[:, 16:24],
            gcol[:, kt_:kt_ + 1], 1.0 / 64, OP.mult, OP.mult), inc=1)
        t4 = k.op("vector", lambda E, kt_=kt_: E.tensor_scalar(
            wmd_sb[:, kt_ * 16 + 8:kt_ * 16 + 16], wrow[:, 24:32],
            gcol[:, kt_:kt_ + 1], 1.0 / 64, OP.mult, OP.mult), inc=1)
        k.set("wsgl", t4)
    mw0_f = k.buf("mw0_f", [128, 128], F32)
    mw0_b = k.buf("mw0_b", [128, 128], BF16)
    mw1_b = k.buf("mw1_b", [128, 128], BF16)
    mw1T_b = k.buf("mw1T_b", [128, 128], BF16)
    mw1_f = k.buf("mw1_f", [128, 128], F32)
    t1 = k.dma("sync", "mw0", mw0_f[:, :], io["mem_w0"][:, :])
    t2 = k.dma("sync", "mw1", mw1_f[:, :], io["mem_w1"][:, :])
    tc0 = k.op("vector", lambda E: E.tensor_copy(mw0_b[:, :], mw0_f[:, :]),
               waits=[t1], inc=1)
    tc1 = k.op("vector", lambda E: E.tensor_copy(mw1_b[:, :], mw1_f[:, :]),
               waits=[t2], inc=1)
    tpw = k.op("tensor", lambda E: E.transpose(pbb[0][:, :], mw1_b[:, :],
                                               ident_bf[:, :]),
               waits=[tc1, k.get(("pbf", 0)), k.get("ident")], inc=1)
    tew = k.op("vector", lambda E: E.tensor_copy(mw1T_b[:, :], pbb[0][:, :]),
               waits=[tpw], inc=1)
    k.set(("pbf", 0), tew)
    gammaP1 = k.buf("gammaP1", [128, HEADS], F32)
    gl = []
    for h in range(HEADS):
        gl.append(k.dma("sync", "bias", gammaP1[:, h:h + 1],
                        io["mh_gamma"][h, 0, :].rearrange("(p o) -> p o", o=1)))
    tgp = k.op("vector", lambda E: E.tensor_scalar_add(
        gammaP1[:, :], gammaP1[:, :], 1.0), waits=[k.dtok("bias")], inc=1)
    cent8 = k.buf("cent8", [8, 1024], F32)
    centT = k.buf("centT", [128, 64], F32)
    tcl = k.dma("sync", "cent", cent8[:, :], io["centroids"][:, :])
    for dt_ in range(DT):
        tp = k.op("tensor", lambda E, dt_=dt_: E.transpose(
            pbt[dt_ % 2][:, 0:8], cent8[:, dt_ * 128:(dt_ + 1) * 128],
            ident_f[0:8, 0:8]),
            waits=[tcl, k.get(("pbtf", dt_ % 2)), k.get("ident")], inc=1)
        te = k.op("vector", lambda E, dt_=dt_: E.tensor_copy(
            centT[:, dt_ * 8:(dt_ + 1) * 8], pbt[dt_ % 2][:, 0:8]),
            waits=[tp], inc=1)
        k.set(("pbtf", dt_ % 2), te)
    ones_b = k.buf("ones_b", [128, 1], BF16)
    k.op("gpsimd", lambda E: E.memset(ones_b[:, :], 1.0), inc=1)
    prep_v = k.tok("vector")
    prep_t = k.tok("tensor")
    prep_g = k.tok("gpsimd")

    for sl in range(2):
        k.set(("x5", sl), [prep_v, prep_t])
    for tt in range(T // 128):
        sl = tt % 2
        tload = k.dma("sync", "x5%d" % sl, x5[:, sl * D:(sl + 1) * D],
                      io["x1"][tt * 128:(tt + 1) * 128, :],
                      waits=[k.get(("x5", sl))])
        xc = k.buf("xncol5", [128, 12], F32)
        tsum = k.op("vector", lambda E, sl=sl: E.tensor_reduce(
            xc[:, 0:1], x5[:, sl * D:(sl + 1) * D], AX, OP.add),
            waits=[tload, k.get("xc5")], inc=1)
        tsq = k.op("scalar", lambda E, sl=sl: E.activation(
            p5sq[:, :], x5[:, sl * D:(sl + 1) * D],
            AF.Square, accum_out=xc[:, 1:2]),
            waits=[tload, tsum, k.get("p5sq")], inc=1)
        tmv = k.op("vector", lambda E: E.tensor_scalar(
            xc[:, 2:4], xc[:, 0:2], 1.0 / D, None, OP.mult),
            waits=[tsq, tsum], inc=1)
        tms = k.op("vector", lambda E: E.tensor_mul(
            xc[:, 4:5], xc[:, 2:3], xc[:, 2:3]), waits=[tmv], inc=1)
        tvar = k.op("vector", lambda E: E.tensor_sub(
            xc[:, 5:6], xc[:, 3:4], xc[:, 4:5]), waits=[tms], inc=1)
        tsd = k.op("scalar", lambda E: E.activation(
            xc[:, 6:7], xc[:, 5:6], AF.Sqrt, bias=kcon[:, 0:1]),
            waits=[tvar, k.get("kcon")], inc=1)
        trs = k.op("vector", lambda E: E.reciprocal(
            xc[:, 7:8], xc[:, 6:7]), waits=[tsd], inc=1)
        txn = k.op("vector", lambda E, sl=sl: E.tensor_scalar(
            xn1f[:, :], x5[:, sl * D:(sl + 1) * D],
            xc[:, 2:3], xc[:, 7:8], OP.subtract, OP.mult),
            waits=[trs, k.get("xn1f")], inc=1)
        k.set(("x5", sl), txn)
        tsq2 = k.op("scalar", lambda E: E.activation(
            p5sq[:, :], xn1f[:, :], AF.Square, accum_out=xc[:, 8:9]),
            waits=[txn], inc=1)
        k.set("p5sq", tsq2)
        tsd2 = k.op("scalar", lambda E: E.activation(
            xc[:, 9:10], xc[:, 8:9], AF.Sqrt, scale=1.0 / D, bias=kcon[:, 1:2]),
            waits=[tsq2], inc=1)
        trs2 = k.op("vector", lambda E: E.reciprocal(
            xc[:, 10:11], xc[:, 9:10]), waits=[tsd2], inc=1)
        k.set("xc5", trs2)
        trb = k.op("vector", lambda E: E.tensor_scalar_mul(
            r_b[:, :], xn1f[:, :], xc[:, 10:11]),
            waits=[trs2, k.get("r_b5")], inc=1)
        tcb = k.op("vector", lambda E: E.tensor_copy(xn1b[:, :], xn1f[:, :]),
                   waits=[txn, k.get("xn1b5")], inc=1)
        ttm = k.dma("gpsimd", "stxtm", io["xn1_tm"][tt * 128:(tt + 1) * 128, :],
                    xn1b[:, :], waits=[tcb])
        for dt_ in range(DT):
            ps = pbb[dt_ % 2][:, :]
            tp = k.op("tensor", lambda E, dt_=dt_, ps=ps: E.transpose(
                ps, xn1b[:, dt_ * 128:(dt_ + 1) * 128], ident_bf[:, :]),
                waits=[tcb, k.get(("pbf", dt_ % 2))], inc=1)
            tev = k.op("vector", lambda E, dt_=dt_, ps=ps: E.tensor_copy(
                p5st[:, (dt_ % 4) * 128:(dt_ % 4 + 1) * 128], ps),
                waits=[tp, k.get(("p5st", dt_ % 4))], inc=1)
            k.set(("pbf", dt_ % 2), tev)
            tst = k.dma("gpsimd", "stxn%d" % (dt_ % 4),
                        io["xn1T"][dt_, :, tt * 128:(tt + 1) * 128],
                        p5st[:, (dt_ % 4) * 128:(dt_ % 4 + 1) * 128],
                        waits=[tev])
            k.set(("p5st", dt_ % 4), tst)
        k.set("xn1b5", [k.tok("tensor"), ttm])
        for dt_ in range(DT):
            ps = pbb[dt_ % 2][:, :]
            tp = k.op("tensor", lambda E, dt_=dt_, ps=ps: E.transpose(
                ps, r_b[:, dt_ * 128:(dt_ + 1) * 128], ident_bf[:, :]),
                waits=[trb, k.get(("pbf", dt_ % 2))], inc=1)
            tev = k.op("vector", lambda E, dt_=dt_, ps=ps: E.tensor_copy(
                p5st[:, (dt_ % 4) * 128:(dt_ % 4 + 1) * 128], ps),
                waits=[tp, k.get(("p5st", dt_ % 4))], inc=1)
            k.set(("pbf", dt_ % 2), tev)
            tst = k.dma("gpsimd", "strt%d" % (dt_ % 4),
                        io["rT"][dt_, :, tt * 128:(tt + 1) * 128],
                        p5st[:, (dt_ % 4) * 128:(dt_ % 4 + 1) * 128],
                        waits=[tev])
            k.set(("p5st", dt_ % 4), tst)
        k.set("r_b5", k.tok("tensor"))
        for dt_ in range(DT):
            ps = pbt[dt_ % 2][:, :]
            tp = k.op("tensor", lambda E, dt_=dt_, ps=ps: E.transpose(
                ps, xn1f[:, dt_ * 128:(dt_ + 1) * 128], ident_f[:, :]),
                waits=[txn, k.get(("pbtf", dt_ % 2))], inc=1)
            tev = k.op("vector", lambda E, dt_=dt_, ps=ps: E.tensor_copy(
                xtf[:, (dt_ % 2) * 128:(dt_ % 2 + 1) * 128], ps),
                waits=[tp, k.get(("xtf", dt_ % 2))], inc=1)
            k.set(("pbtf", dt_ % 2), tev)
            tmm = k.op("tensor", lambda E, dt_=dt_: E.matmul(
                pb[3][:, 0:8], xtf[:, (dt_ % 2) * 128:(dt_ % 2 + 1) * 128],
                centT[:, dt_ * 8:(dt_ + 1) * 8],
                start=(dt_ == 0), stop=(dt_ == DT - 1)),
                waits=[tev] + ([k.get(("pbank", 3)), prep_v] if dt_ == 0 else []),
                inc=1 if dt_ == DT - 1 else 0)
            k.set(("xtf", dt_ % 2), tmm)
        taf = k.op("vector", lambda E: E.tensor_copy(affsb[:, :], pb[3][:, 0:8]),
                   waits=[k.tok("tensor"), k.get("affsb")], inc=1)
        k.set(("pbank", 3), taf)
        tsa = k.dma("gpsimd", "staff", io["aff"][tt * 128:(tt + 1) * 128, :],
                    affsb[:, :], waits=[taf])
        k.set("affsb", tsa)
    p5_done = [k.dtok("stxn%d" % i) for i in range(4)] + \
              [k.dtok("strt%d" % i) for i in range(4)] + \
              [k.dtok("staff"), k.dtok("stxtm")]
    return p5_done


def build_l1a_p6a(k, io, p5_done):
    nc = k.nc
    pb = k.pb
    wkv_sb = k.bufs["wkv_sb"]
    wq_sb = k.bufs["wq_sb"]
    wsg_sb = k.bufs["wsg_sb"]
    wmd_sb = k.bufs["wmd_sb"]
    rT_c = k.bufs["rT_c"]
    rTs_c = k.bufs["rTs_c"]
    lrst = k.bufs["lrst"]
    cm_sb = k.bufs["cm_sb"]
    a6st = k.bufs["a6st"]
    md_sb = k.bufs["md_sb"]

    def wkv_ap(kt, mt):
        return wkv_sb[:, kt * 2048 + mt * 128: kt * 2048 + (mt + 1) * 128]

    def wq_ap(kt, mt):
        return wq_sb[:, kt * 1024 + mt * 128: kt * 1024 + (mt + 1) * 128]

    first6 = p5_done + [k.tok("vector"), k.tok("scalar"), k.tok("tensor")]
    k.set("rT_c", first6)
    k.set("rTs_c", first6)
    k.set("cm0", first6)

    for c in range(NCHUNK):
        rl = []
        tz = None
        if c == NCHUNK - 1:
            tz = k.op("gpsimd", lambda E: E.memset(rTs_c[:, :], 0.0),
                      waits=[k.get("rTs_c")], inc=1)
        for dt_ in range(DT):
            rl.append(k.dma("sync", "rtc", rT_c[:, dt_ * TCH:(dt_ + 1) * TCH],
                            io["rT"][dt_, :, c * TCH:(c + 1) * TCH],
                            waits=[k.get("rT_c")]))
            ncols = TCH if c < NCHUNK - 1 else TCH - 63
            rl.append(k.dma("sync", "rtc", rTs_c[:, dt_ * TCH: dt_ * TCH + ncols],
                            io["rT"][dt_, :, c * TCH + 63: c * TCH + 63 + ncols],
                            waits=[k.get("rTs_c"), tz]))
        for mt in range(16):
            bank = pb[mt % 2]
            for kt in range(DT):
                k.op("tensor", lambda E, mt=mt, kt=kt, bank=bank: E.matmul(
                    bank[:, :], wkv_ap(kt, mt), rT_c[:, kt * TCH:(kt + 1) * TCH],
                    start=(kt == 0), stop=(kt == DT - 1)),
                    waits=(rl + [k.get(("pbank", mt % 2))]) if kt == 0 else [],
                    inc=1 if kt == DT - 1 else 0)
            mmt = k.tok("tensor")
            te = k.op("scalar", lambda E, mt=mt, bank=bank: E.copy(
                a6st[:, (mt % 2) * 512:(mt % 2 + 1) * 512], bank[:, :]),
                waits=[mmt, k.get(("a6st", mt % 2))], inc=1)
            ts = k.dma("gpsimd", "stkv%d" % (mt % 2),
                       io["kvT"][mt, :, c * TCH:(c + 1) * TCH],
                       a6st[:, (mt % 2) * 512:(mt % 2 + 1) * 512], waits=[te])
            k.set(("a6st", mt % 2), ts)
            k.set(("pbank", mt % 2), te)
        for mt in range(8):
            bank = pb[mt % 2]
            for kt in range(DT):
                k.op("tensor", lambda E, mt=mt, kt=kt, bank=bank: E.matmul(
                    bank[:, :], wq_ap(kt, mt), rTs_c[:, kt * TCH:(kt + 1) * TCH],
                    start=(kt == 0), stop=(kt == DT - 1)),
                    waits=[k.get(("pbank", mt % 2))] if kt == 0 else [],
                    inc=1 if kt == DT - 1 else 0)
            mmt = k.tok("tensor")
            te = k.op("scalar", lambda E, mt=mt, bank=bank: E.copy(
                a6st[:, (mt % 2) * 512:(mt % 2 + 1) * 512], bank[:, :]),
                waits=[mmt, k.get(("a6st", mt % 2))], inc=1)
            ts = k.dma("gpsimd", "stkv%d" % (mt % 2),
                       io["qT"][mt, :, c * TCH:(c + 1) * TCH],
                       a6st[:, (mt % 2) * 512:(mt % 2 + 1) * 512], waits=[te])
            k.set(("a6st", mt % 2), ts)
            k.set(("pbank", mt % 2), te)
        for which, rhs, dst in (("lr", rT_c, io["lrq"]), ("gt", rTs_c, io["gateq"])):
            off = 0 if which == "lr" else 8
            for kt in range(DT):
                k.op("tensor", lambda E, kt=kt, off=off, rhs=rhs: E.matmul(
                    pb[2][0:8, :], wsg_sb[:, kt * 16 + off: kt * 16 + off + 8],
                    rhs[:, kt * TCH:(kt + 1) * TCH],
                    start=(kt == 0), stop=(kt == DT - 1)),
                    waits=[k.get(("pbank", 2))] if kt == 0 else [],
                    inc=1 if kt == DT - 1 else 0)
            mmt = k.tok("tensor")
            te = k.op("scalar", lambda E: E.activation(
                lrst[:, :], pb[2][0:8, :], AF.Sigmoid),
                waits=[mmt, k.get("a6lr")], inc=1)
            ts = k.dma("gpsimd", "stlr", dst[:, c * TCH:(c + 1) * TCH],
                       lrst[:, :], waits=[te])
            k.set("a6lr", ts)
            k.set(("pbank", 2), te)
        for dt_ in range(DT):
            tr = k.op("vector", lambda E, dt_=dt_, c=c: E.tensor_reduce(
                cm_sb[:, dt_ * 64 + c * 8: dt_ * 64 + (c + 1) * 8],
                rT_c[:, dt_ * TCH:(dt_ + 1) * TCH].rearrange(
                    "p (s i) -> p s i", s=8),
                AX, OP.add), waits=rl + [k.get("cm0")], inc=1)
        k.set("rT_c", [k.tok("tensor"), k.tok("vector")])
        k.set("rTs_c", k.tok("tensor"))
    cm_cast = k.op("vector", lambda E: E.tensor_copy(
        k.bufs["cm_b"][:, :], k.bufs["cm_sb"][:, :]),
        waits=[k.tok("vector")], inc=1)
    cm_done = cm_cast
    for kt in range(DT):
        k.op("tensor", lambda E, kt=kt: E.matmul(
            pb[2][0:16, 0:64], wmd_sb[:, kt * 16:(kt + 1) * 16],
            k.bufs["cm_b"][:, kt * 64:(kt + 1) * 64],
            start=(kt == 0), stop=(kt == DT - 1)),
            waits=[cm_done, k.get(("pbank", 2))] if kt == 0 else [],
            inc=1 if kt == DT - 1 else 0)
    mmt = k.tok("tensor")
    te = k.op("scalar", lambda E: E.activation(
        md_sb[:, :], pb[2][0:16, 0:64], AF.Sigmoid), waits=[mmt], inc=1)
    k.set(("pbank", 2), te)
    tz0 = k.op("gpsimd", lambda E: E.memset(md_sb[0:8, 0:1], 0.0),
               waits=[te], inc=1)
    ts = k.dma("gpsimd", "stmd", io["md0"][:, :], md_sb[:, :], waits=[tz0])
    for r in range(16):
        k.dma("gpsimd", "stmd",
              io["grow"][r, :].rearrange("(f i) -> f i", i=64),
              bass.AP(io["md0"], r * 64, [[0, 128], [1, 64]]),
              waits=[ts])
    return [k.dtok("stmd"), k.dtok("stkv0"), k.dtok("stkv1"), k.dtok("stlr")]


def build_l1a_p6b(k, io, p6a_done):
    nc = k.nc
    pb = k.pb
    pbt = k.pbt
    pbb = k.pbb
    ident_bf = k.bufs["ident_bf"]
    kcon = k.bufs["kcon"]
    kT = k.bufs["kT"]; vT = k.bufs["vT"]; szb = k.bufs["szb"]
    Gb = k.bufs["Gb"]; dzb = k.bufs["dzb"]; lrb = k.bufs["lrb"]
    gm_t = k.bufs["gm_t"]; gd_t = k.bufs["gd_t"]
    s0 = k.bufs["s0"]; s1s = k.bufs["s1s"]; msc = k.bufs["msc"]
    qTh = k.bufs["qTh"]; blk = k.bufs["blk"]; sg_f = k.bufs["sg_f"]
    tmpf = k.bufs["tmpf"]; wi_sl = k.bufs["wi_sl"]; vals_b = k.bufs["vals_b"]
    sqb = k.bufs["sqb"]; scale_bc = k.bufs["scale_bc"]; rowq = k.bufs["rowq"]
    mw0_f = k.bufs["mw0_f"]; mw0_b = k.bufs["mw0_b"]
    mw1_b = k.bufs["mw1_b"]; mw1T_b = k.bufs["mw1T_b"]; mw1_f = k.bufs["mw1_f"]
    gammaP1 = k.bufs["gammaP1"]; ones_b = k.bufs["ones_b"]
    sxb = k.buf("sxb", [128, 2 * 64], BF16)

    k.set("hcyc", p6a_done + [k.tok("tensor"), k.tok("vector"), k.tok("scalar")])
    for h in range(HEADS):
        hc = k.get("hcyc")
        tk = k.dma("sync", "ldk", kT[:, :], io["kvT"][h, :, :], waits=[hc])
        tv = k.dma("sync", "ldv", vT[:, :], io["kvT"][8 + h, :, :], waits=[hc])
        tlr = k.dma("gpsimd", "ldlr", lrb[:, :],
                    bass.AP(io["lrq"], h * T, [[0, 128], [1, T]]), waits=[hc])
        tgm = k.dma("gpsimd", "ldgm", gm_t[:, :],
                    bass.AP(io["grow"], h * 8192, [[0, 128], [1, 8192]]),
                    waits=[hc])
        tgd = k.dma("gpsimd", "ldgd", gd_t[:, :],
                    bass.AP(io["grow"], (8 + h) * 8192, [[0, 128], [1, 8192]]),
                    waits=[hc])
        t1m = k.op("vector", lambda E: E.tensor_scalar(
            gd_t[:, :], gd_t[:, :], -1.0, 1.0, OP.mult, OP.add),
            waits=[tgd], inc=1)
        tzc = k.op("gpsimd", lambda E: E.memset(
            bass.AP(gd_t, 0, [[8192, 128], [64, 128]]), 0.0),
            waits=[t1m], inc=1)

        for j in range(NCHUNK):
            sl = j % 2
            zmm = k.op("tensor", lambda E, j=j: E.matmul(
                pb[0][:, :], mw0_b[:, :], kT[:, j * TCH:(j + 1) * TCH],
                start=True, stop=True),
                waits=[tk, k.get(("pbank", 0))], inc=1)
            tsg = k.op("scalar", lambda E, sl=sl: E.activation(
                sg_f[:, sl * TCH:(sl + 1) * TCH], pb[0][:, :], AF.Sigmoid),
                waits=[zmm, k.get(("sgf", sl))], inc=1)
            tsz = k.op("vector", lambda E, j=j, sl=sl: E.tensor_mul(
                szb[:, j * TCH:(j + 1) * TCH], pb[0][:, :],
                sg_f[:, sl * TCH:(sl + 1) * TCH]),
                waits=[tsg], inc=1)
            tt1 = k.op("vector", lambda E, sl=sl: E.tensor_scalar(
                tmpf[:, 0:TCH], sg_f[:, sl * TCH:(sl + 1) * TCH],
                -1.0, 1.0, OP.mult, OP.add), waits=[tsg, k.get("tmpf_a")], inc=1)
            tt2 = k.op("vector", lambda E: E.tensor_mul(
                tmpf[:, TCH:2 * TCH], pb[0][:, :], tmpf[:, 0:TCH]),
                waits=[tt1], inc=1)
            tds = k.op("vector", lambda E, j=j, sl=sl: E.scalar_tensor_tensor(
                dzb[:, j * TCH:(j + 1) * TCH], tmpf[:, TCH:2 * TCH], 1.0,
                sg_f[:, sl * TCH:(sl + 1) * TCH], OP.add, OP.mult),
                waits=[tt2], inc=1)
            k.set(("pbank", 0), tt2)
            k.set(("sgf", sl), tds)
            pmm = k.op("tensor", lambda E, j=j: E.matmul(
                pb[1][:, :], mw1_b[:, :], szb[:, j * TCH:(j + 1) * TCH],
                start=True, stop=True),
                waits=[tsz, k.get(("pbank", 1))], inc=1)
            te1 = k.op("vector", lambda E, j=j: E.tensor_sub(
                tmpf[:, 0:TCH], vT[:, j * TCH:(j + 1) * TCH], pb[1][:, :]),
                waits=[pmm, tv, tds, tt2], inc=1)
            tG = k.op("vector", lambda E, j=j: E.scalar_tensor_tensor(
                Gb[:, j * TCH:(j + 1) * TCH], tmpf[:, 0:TCH], C0G,
                lrb[:, j * TCH:(j + 1) * TCH], OP.mult, OP.mult),
                waits=[tlr, te1], inc=1)
            k.set("tmpf_a", tG)
            k.set(("pbank", 1), te1)
            dmm = k.op("tensor", lambda E, j=j: E.matmul(
                pb[2][:, :], mw1T_b[:, :], Gb[:, j * TCH:(j + 1) * TCH],
                start=True, stop=True),
                waits=[tG, k.get(("pbank", 2))], inc=1)
            tdz = k.op("vector", lambda E, j=j: E.tensor_mul(
                dzb[:, j * TCH:(j + 1) * TCH], pb[2][:, :],
                dzb[:, j * TCH:(j + 1) * TCH]),
                waits=[dmm, tds], inc=1)
            k.set(("pbank", 2), tdz)
        zpd_v = k.tok("vector")

        for b in range(T // 128):
            bs = (b % 2) * 4
            evs = []
            for wi_, src in ((0, kT), (1, szb), (2, Gb), (3, dzb)):
                ps = pbb[wi_ % 2][:, :]
                tp = k.op("tensor", lambda E, b=b, src=src, ps=ps: E.transpose(
                    ps, src[:, b * 128:(b + 1) * 128], ident_bf[:, :]),
                    waits=[zpd_v, k.get(("pbf", wi_ % 2))], inc=1)
                tev = k.op("vector", lambda E, bs=bs, wi_=wi_, ps=ps: E.tensor_copy(
                    blk[:, (bs + wi_) * 128:(bs + wi_ + 1) * 128], ps),
                    waits=[tp, k.get(("blk", bs + wi_))], inc=1)
                k.set(("pbf", wi_ % 2), tev)
                evs.append(tev)
            for half in range(2):
                i = 2 * b + half
                hb = half * 64
                g0 = k.op("tensor", lambda E, bs=bs, hb=hb: E.matmul(
                    pb[3][:, 0:128],
                    blk[hb:hb + 64, (bs + 0) * 128:(bs + 1) * 128],
                    blk[hb:hb + 64, (bs + 3) * 128:(bs + 4) * 128],
                    start=True, stop=True),
                    waits=evs + [k.get(("pbank", 3))], inc=1)
                g0e = k.op("scalar", lambda E, i=i: E.copy(
                    bass.AP(s0, i, [[8192, 128], [64, 128]]), pb[3][:, 0:128]),
                    waits=[g0, p6a_done if (h == 0 and i == 0) else None], inc=1)
                k.set(("pbank", 3), g0e)
                g1 = k.op("tensor", lambda E, bs=bs, hb=hb: E.matmul(
                    pb[2][:, 0:128],
                    blk[hb:hb + 64, (bs + 1) * 128:(bs + 2) * 128],
                    blk[hb:hb + 64, (bs + 2) * 128:(bs + 3) * 128],
                    start=True, stop=True),
                    waits=[k.get(("pbank", 2))], inc=1)
                g1e = k.op("scalar", lambda E, i=i: E.copy(
                    bass.AP(s1s, i, [[8192, 128], [64, 128]]), pb[2][:, 0:128]),
                    waits=[g1, p6a_done if (h == 0 and i == 0) else None], inc=1)
                k.set(("pbank", 2), g1e)
            k.set(("blk", bs + 0), k.tok("tensor"))
            k.set(("blk", bs + 1), k.tok("tensor"))
            k.set(("blk", bs + 2), k.tok("tensor"))
            k.set(("blk", bs + 3), k.tok("tensor"))
        grads_s = k.tok("scalar")
        grads_t = k.tok("tensor")

        m0 = k.op("vector", lambda E: E.tensor_tensor_scan(
            msc[:, :], gm_t[:, :], s0[:, :], 0.0, OP.mult, OP.add),
            waits=[grads_s, grads_t, tgm], inc=1)
        u0 = k.op("vector", lambda E: E.tensor_tensor_scan(
            s0[:, :], gd_t[:, :], msc[:, :], 0.0, OP.mult, OP.add),
            waits=[m0, t1m, tzc], inc=1)
        m1 = k.op("vector", lambda E: E.tensor_tensor_scan(
            msc[:, :], gm_t[:, :], s1s[:, :], 0.0, OP.mult, OP.add),
            waits=[u0], inc=1)
        u1 = k.op("vector", lambda E: E.tensor_tensor_scan(
            s1s[:, :], gd_t[:, :], msc[:, :], 0.0, OP.mult, OP.add),
            waits=[m1], inc=1)

        tql = k.dma("sync", "ldq", qTh[:, :], io["qT"][h, :, :],
                    waits=[u1])
        for i in range(NCK):
            sl = i % 2
            w0i = k.op("vector", lambda E, i=i, sl=sl: E.tensor_add(
                wi_sl[:, sl * 128:(sl + 1) * 128], mw0_f[:, :],
                bass.AP(s0, i, [[8192, 128], [64, 128]])),
                waits=[u0, k.get(("wi0", sl))], inc=1)
            w1i = k.op("vector", lambda E, i=i, sl=sl: E.tensor_add(
                wi_sl[:, (2 + sl) * 128:(3 + sl) * 128], mw1_f[:, :],
                bass.AP(s1s, i, [[8192, 128], [64, 128]])),
                waits=[u1, k.get(("wi1", sl))], inc=1)
            xmm = k.op("tensor", lambda E, i=i, sl=sl: E.matmul(
                pb[sl][:, 0:64], wi_sl[:, sl * 128:(sl + 1) * 128],
                qTh[:, i * 64:(i + 1) * 64], start=True, stop=True),
                waits=[w0i, tql, k.get(("pbank", sl))], inc=1)
            tsx = k.op("scalar", lambda E, sl=sl: E.activation(
                sxb[:, sl * 64:(sl + 1) * 64], pb[sl][:, 0:64], AF.Silu),
                waits=[xmm, k.get(("sxb", sl))], inc=1)
            k.set(("pbank", sl), tsx)
            k.set(("wi0", sl), xmm)
            vmm = k.op("tensor", lambda E, i=i, sl=sl: E.matmul(
                pb[2 + sl][:, 0:64], wi_sl[:, (2 + sl) * 128:(3 + sl) * 128],
                sxb[:, sl * 64:(sl + 1) * 64], start=True, stop=True),
                waits=[w1i, tsx, k.get(("pbank", 2 + sl))], inc=1)
            vev = k.op("vector", lambda E, i=i, sl=sl: E.tensor_copy(
                vals_b[:, i * 64:(i + 1) * 64], pb[2 + sl][:, 0:64]),
                waits=[vmm], inc=1)
            k.set(("pbank", 2 + sl), vev)
            k.set(("wi1", sl), vmm)
            k.set(("sxb", sl), vmm)
        ret_v = k.tok("vector")

        tsq = k.op("scalar", lambda E: E.activation(
            sqb[:, :], vals_b[:, :], AF.Square), waits=[ret_v], inc=1)
        for j in range(NCHUNK):
            smm = k.op("tensor", lambda E, j=j: E.matmul(
                pb[j % 2][0:1, :], ones_b[:, :], sqb[:, j * TCH:(j + 1) * TCH],
                start=True, stop=True),
                waits=[tsq, k.get(("pbank", j % 2))], inc=1)
            sev = k.op("scalar", lambda E, j=j: E.copy(
                rowq[0:1, 0:TCH], pb[j % 2][0:1, :]),
                waits=[smm, k.get("rowA")], inc=1)
            k.set(("pbank", j % 2), sev)
            rln = k.op("scalar", lambda E: E.activation(
                rowq[0:1, TCH:2 * TCH], rowq[0:1, 0:TCH], AF.Ln,
                scale=1.0 / DH, bias=kcon[0:1, 1:2]),
                waits=[sev, k.get("rowB")], inc=1)
            rex = k.op("scalar", lambda E: E.activation(
                rowq[0:1, 0:TCH], rowq[0:1, TCH:2 * TCH], AF.Exp, scale=-0.5),
                waits=[rln], inc=1)
            tgl = k.dma("sync", "ldgt", rowq[0:1, 2 * TCH:3 * TCH],
                        io["gateq"][h, j * TCH:(j + 1) * TCH].rearrange(
                            "(o n) -> o n", o=1),
                        waits=[k.get("rowq_g")])
            tscr = k.op("vector", lambda E: E.tensor_mul(
                rowq[0:1, TCH:2 * TCH], rowq[0:1, 0:TCH],
                rowq[0:1, 2 * TCH:3 * TCH]),
                waits=[rex, tgl], inc=1)
            k.set("rowq_g", tscr)
            k.set("rowA", tscr)
            tsr = k.dma("gpsimd", "scrow",
                        io["scrow"][0, j * TCH:(j + 1) * TCH].rearrange(
                            "(o n) -> o n", o=1),
                        rowq[0:1, TCH:2 * TCH], waits=[tscr])
            k.set("rowB", tsr)
        tsb = k.dma("gpsimd", "scrow", scale_bc[:, :],
                    bass.AP(io["scrow"], 0, [[0, 128], [1, T]]),
                    waits=[k.dtok("scrow"), u1])
        ep1 = k.op("vector", lambda E: E.tensor_mul(
            vals_b[:, :], vals_b[:, :], scale_bc[:, :]),
            waits=[tsb, tsq], inc=1)
        ep2 = k.op("vector", lambda E, h=h: E.tensor_scalar_mul(
            vals_b[:, :], vals_b[:, :], gammaP1[:, h:h + 1]), waits=[ep1], inc=1)
        tvs = k.dma("gpsimd", "stvals", io["valsT"][h, :, :], vals_b[:, :],
                    waits=[ep2])
        k.set("hcyc", [tvs, k.tok("tensor"), k.tok("vector"), k.tok("scalar")])
    return k.get("hcyc")


def build_l1a_p6c(k, io, heads_done):
    nc = k.nc
    pb = k.pb
    wc_sb = k.bufs["wc_sb"]
    vals_c = k.bufs["vals_c"]
    c6st = k.bufs["c6st"]
    wl = []
    for kh in range(HEADS):
        wl.append(k.dma("gpsimd", "ldwc", wc_sb[:, kh * 1024:(kh + 1) * 1024],
                        io["w_combine"][kh * 128:(kh + 1) * 128, :],
                        waits=[heads_done]))
    for c in range(NCHUNK):
        vl = []
        for kh in range(HEADS):
            vl.append(k.dma("sync", "ldvc", vals_c[:, kh * TCH:(kh + 1) * TCH],
                            io["valsT"][kh, :, c * TCH:(c + 1) * TCH],
                            waits=[heads_done if c == 0 else k.get("vals_c")]))
        for m in range(DT):
            bank = pb[m % 2]
            for kh in range(HEADS):
                k.op("tensor", lambda E, m=m, kh=kh, bank=bank: E.matmul(
                    bank[:, :],
                    wc_sb[:, kh * 1024 + m * 128: kh * 1024 + (m + 1) * 128],
                    vals_c[:, kh * TCH:(kh + 1) * TCH],
                    start=(kh == 0), stop=(kh == HEADS - 1)),
                    waits=(wl + vl + [k.get(("pbank", m % 2))]) if kh == 0 else [],
                    inc=1 if kh == HEADS - 1 else 0)
            mmt = k.tok("tensor")
            te = k.op("vector", lambda E, m=m, bank=bank: E.tensor_copy(
                c6st[:, (m % 2) * TCH:(m % 2 + 1) * TCH], bank[:, :]),
                waits=[mmt, k.get(("c6st", m % 2))], inc=1)
            ts = k.dma("gpsimd", "sts1%d" % (m % 2),
                       io["s1T"][m, :, c * TCH:(c + 1) * TCH],
                       c6st[:, (m % 2) * TCH:(m % 2 + 1) * TCH], waits=[te])
            k.set(("c6st", m % 2), ts)
            k.set(("pbank", m % 2), te)
        k.set("vals_c", k.tok("tensor"))
    return [k.dtok("sts1%d" % i) for i in range(2)]


def build_l1a_program():
    nc = bass.Bass("TRN2", target_bir_lowering=False)
    io = {
        "x1": _dram(nc, "x1", (T, D), F32, "ExternalInput"),
        "g_s": _dram(nc, "g_s", (D,), F32, "ExternalInput"),
        "g_r": _dram(nc, "g_r", (D,), F32, "ExternalInput"),
        "w_kv": _dram(nc, "w_kv", (D, 2048), F32, "ExternalInput"),
        "w_q": _dram(nc, "w_q", (D, D), F32, "ExternalInput"),
        "w_step": _dram(nc, "w_step", (D, 8), F32, "ExternalInput"),
        "w_mom": _dram(nc, "w_mom", (D, 8), F32, "ExternalInput"),
        "w_decay": _dram(nc, "w_decay", (D, 8), F32, "ExternalInput"),
        "w_gate": _dram(nc, "w_gate", (D, 8), F32, "ExternalInput"),
        "w_combine": _dram(nc, "w_combine", (D, D), F32, "ExternalInput"),
        "mh_gamma": _dram(nc, "mh_gamma", (8, 1, 128), F32, "ExternalInput"),
        "mem_w0": _dram(nc, "mem_w0", (128, 128), F32, "ExternalInput"),
        "mem_w1": _dram(nc, "mem_w1", (128, 128), F32, "ExternalInput"),
        "centroids": _dram(nc, "centroids", (8, D), F32, "ExternalInput"),
        "rT": _dram(nc, "rT", (DT, 128, T), BF16, "Internal"),
        "kvT": _dram(nc, "kvT", (16, 128, T), BF16, "Internal"),
        "qT": _dram(nc, "qT", (8, 128, T), BF16, "Internal"),
        "lrq": _dram(nc, "lrq", (8, T), F32, "Internal"),
        "gateq": _dram(nc, "gateq", (8, T), F32, "Internal"),
        "md0": _dram(nc, "md0", (16, 64), F32, "Internal"),
        "grow": _dram(nc, "grow", (16, 8192), F32, "Internal"),
        "scrow": _dram(nc, "scrow", (1, T), F32, "Internal"),
        "valsT": _dram(nc, "valsT", (8, 128, T), BF16, "Internal"),
        "xn1_tm": _dram(nc, "xn1_tm", (T, D), BF16, "Internal"),
        "s1T": _dram(nc, "s1T", (DT, 128, T), F32, "ExternalOutput"),
        "xn1T": _dram(nc, "xn1T", (DT, 128, T), BF16, "ExternalOutput"),
        "aff": _dram(nc, "aff", (T, 8), F32, "ExternalOutput"),
    }
    k = K(nc)
    k.pb = [nc.alloc_psum_tensor("pb%d" % i, [128, 512], F32) for i in range(4)]
    k.pbt = [nc.alloc_psum_tensor("pbt%d" % i, [128, 128], F32) for i in range(2)]
    k.pbb = [nc.alloc_psum_tensor("pbb%d" % i, [128, 128], BF16) for i in range(2)]
    emit_identities(k)
    p5_done = build_l1a(k, io)
    p6a_done = build_l1a_p6a(k, io, p5_done)
    heads_done = build_l1a_p6b(k, io, p6a_done)
    last = build_l1a_p6c(k, io, heads_done)
    k.op("sync", lambda E: E.nop(),
         waits=last + p5_done + [k.dtok("staff")])
    k.emit()
    return nc


def l1a_in_map(inp, x1_core):
    return {
        "x1": np.ascontiguousarray(x1_core),
        "g_s": inp["L1_attn_store_norm_g"],
        "g_r": inp["L1_attn_retrieve_norm_g"],
        "w_kv": inp["L1_attn_w_kv"], "w_q": inp["L1_attn_w_q"],
        "w_step": inp["L1_attn_w_step"], "w_mom": inp["L1_attn_w_mom"],
        "w_decay": inp["L1_attn_w_decay"], "w_gate": inp["L1_attn_w_gate"],
        "w_combine": inp["L1_attn_w_combine"],
        "mh_gamma": inp["L1_attn_mh_gamma"],
        "mem_w0": inp["L1_attn_mem_w0"], "mem_w1": inp["L1_attn_mem_w1"],
        "centroids": inp["L1_ffn_centroids"],
    }


def build_l1b(k, io):
    nc = k.nc
    pb = k.pb
    pbt = k.pbt
    pbb = k.pbb
    ident_bf = k.bufs["ident_bf"]
    ident_f = k.bufs["ident_f"]
    kcon = k.bufs["kcon"]
    HC = 512
    NHC = T // HC

    w1_sb = k.buf("w1_sb", [128, DT * FF], BF16)
    w2_sb = k.buf("w2_sb", [128, FT * D], BF16)
    xn1c = k.buf("xn1c", [128, DT * HC], BF16)
    xlnc = k.buf("xlnc", [128, DT * HC], BF16)
    hm = k.buf("hm", [128, FT * HC], BF16)
    mask_bc = k.buf("mask_bc", [128, 2 * HC], BF16)
    moeev = k.buf("moeev", [128, 2 * HC], F32)
    b2_sb = k.buf("b2_sb", [8, 1024], BF16)
    lngT = k.buf("lngT", [128, DT * 8], F32)
    lnbT = k.buf("lnbT", [128, DT * 8], F32)
    b1T = k.buf("b1T", [128, FT * 8], F32)
    afft = k.buf("afft", [128, 2 * 8], F32)
    mcol = k.buf("mcol", [128, 4], F32)
    mkb = k.buf("mkb", [128, 2 * 8], BF16)
    mrow = k.buf("mrow", [8, 2 * 128], BF16)
    g8 = k.buf("g8", [8, 4096], F32)
    msk8 = k.overlay("msk8", w1_sb, [8, 512], BF16, byte_off=0)
    moec = k.overlay("moec", w1_sb, [128, 2 * 512], F32, byte_off=1024)
    s1c = k.overlay("s1c", w1_sb, [128, 2 * 512], F32, byte_off=5120)
    xn1P8 = k.overlay("xn1P8", w1_sb, [128, 2 * 512], BF16, byte_off=9216)
    al_bc = k.overlay("al_bc", w1_sb, [128, 512], F32, byte_off=11264)
    g1c = k.overlay("g1c", w1_sb, [128, 2 * 512], F32, byte_off=13312)
    xres8 = k.overlay("xres8", w1_sb, [128, 4 * D], F32, byte_off=17408)

    tb2 = k.dma("gpsimd", "ldb2", b2_sb[:, :], io["b2"][:, :])
    tg8 = k.dma("sync", "ldg8", g8[:, 0:1024], io["ln_g"][:, :])
    tb8 = k.dma("sync", "ldg8", g8[:, 1024:2048], io["ln_b"][:, :])
    for dt_ in range(DT):
        tp = k.op("tensor", lambda E, dt_=dt_: E.transpose(
            pbt[dt_ % 2][:, 0:8], g8[:, dt_ * 128:(dt_ + 1) * 128],
            ident_f[0:8, 0:8]),
            waits=[tg8, tb8, k.get(("pbtf", dt_ % 2)), k.get("ident")], inc=1)
        te = k.op("vector", lambda E, dt_=dt_: E.tensor_copy(
            lngT[:, dt_ * 8:(dt_ + 1) * 8], pbt[dt_ % 2][:, 0:8]),
            waits=[tp], inc=1)
        k.set(("pbtf", dt_ % 2), te)
        tp2 = k.op("tensor", lambda E, dt_=dt_: E.transpose(
            pbt[dt_ % 2][:, 0:8], g8[:, 1024 + dt_ * 128:1024 + (dt_ + 1) * 128],
            ident_f[0:8, 0:8]),
            waits=[k.get(("pbtf", dt_ % 2))], inc=1)
        te2 = k.op("vector", lambda E, dt_=dt_: E.tensor_copy(
            lnbT[:, dt_ * 8:(dt_ + 1) * 8], pbt[dt_ % 2][:, 0:8]),
            waits=[tp2], inc=1)
        k.set(("pbtf", dt_ % 2), te2)
    tb1l = k.dma("sync", "ldg8", g8[:, :],
                 io["b1"][:, :], waits=[k.tok("tensor"), k.tok("vector")])
    for ht in range(FT):
        tp = k.op("tensor", lambda E, ht=ht: E.transpose(
            pbt[ht % 2][:, 0:8], g8[:, ht * 128:(ht + 1) * 128],
            ident_f[0:8, 0:8]),
            waits=[tb1l, k.get(("pbtf", ht % 2))], inc=1)
        te = k.op("vector", lambda E, ht=ht: E.tensor_copy(
            b1T[:, ht * 8:(ht + 1) * 8], pbt[ht % 2][:, 0:8]),
            waits=[tp], inc=1)
        k.set(("pbtf", ht % 2), te)

    for tt in range(T // 128):
        sl = tt % 2
        tl = k.dma("sync", "ldaff%d" % sl, afft[:, sl * 8:(sl + 1) * 8],
                   io["aff"][tt * 128:(tt + 1) * 128, :],
                   waits=[k.get(("afft", sl))])
        tmx = k.op("vector", lambda E, sl=sl: E.reduce_max(
            mcol[:, 0:1], afft[:, sl * 8:(sl + 1) * 8], AX),
            waits=[tl, k.get("mcol")], inc=1)
        tal = k.op("scalar", lambda E: E.activation(
            mcol[:, 1:2], mcol[:, 0:1], AF.Sigmoid), waits=[tmx], inc=1)
        tsa = k.dma("gpsimd", "stal",
                    io["alpha"][tt * 128:(tt + 1) * 128].rearrange(
                        "(p o) -> p o", o=1),
                    mcol[:, 1:2], waits=[tal])
        k.set("mcol", tsa)
        tmk = k.op("vector", lambda E, sl=sl: E.tensor_scalar(
            mkb[:, sl * 8:(sl + 1) * 8], afft[:, sl * 8:(sl + 1) * 8],
            mcol[:, 0:1], None, OP.is_equal),
            waits=[tmx, k.get(("mkb", sl))], inc=1)
        k.set(("afft", sl), tmk)
        tp = k.op("tensor", lambda E, sl=sl: E.transpose(
            pbb[sl][0:8, :], mkb[:, sl * 8:(sl + 1) * 8], ident_bf[:, :]),
            waits=[tmk, k.get(("pbf", sl)), k.get("ident")], inc=1)
        k.set(("mkb", sl), tp)
        tev = k.op("vector", lambda E, sl=sl: E.tensor_copy(
            mrow[:, sl * 128:(sl + 1) * 128], pbb[sl][0:8, :]),
            waits=[tp, k.get(("mrow", sl))], inc=1)
        k.set(("pbf", sl), tev)
        tst = k.dma("gpsimd", "stmk%d" % sl, io["maskT"][:, tt * 128:(tt + 1) * 128],
                    mrow[:, sl * 128:(sl + 1) * 128], waits=[tev])
        k.set(("mrow", sl), tst)
    masks_done = [k.dtok("stmk0"), k.dtok("stmk1"), k.dtok("stal")]

    for e in range(NE):
        w1l = []
        for kt in range(DT):
            w1l.append(k.dma("gpsimd", "ldw1%d" % (kt % 2),
                             w1_sb[:, kt * FF:(kt + 1) * FF],
                             io["w1"][e, kt * 128:(kt + 1) * 128, :],
                             waits=[k.get(("w1sb", kt))]))
        w2l = []
        for k2 in range(FT):
            w2l.append(k.dma("gpsimd", "ldw2%d" % (k2 % 2),
                             w2_sb[:, k2 * D:(k2 + 1) * D],
                             io["w2"][e, k2 * 128:(k2 + 1) * 128, :],
                             waits=[k.get(("w2sb", k2))]))
        for c2 in range(NHC):
            xl = []
            for dt_ in range(DT):
                xl.append(k.dma("sync", "ldxc", xn1c[:, dt_ * HC:(dt_ + 1) * HC],
                                io["xn1T"][dt_, :, c2 * HC:(c2 + 1) * HC],
                                waits=[k.get("xn1c")]))
            msl = c2 % 2
            tmb = k.dma("gpsimd", "ldmb%d" % msl, mask_bc[:, msl * HC:(msl + 1) * HC],
                        bass.AP(io["maskT"], e * T + c2 * HC, [[0, 128], [1, HC]]),
                        waits=[k.get(("mask_bc", msl))] + (masks_done if (e == 0 and c2 < 2) else []))
            xw = []
            for dt_ in range(DT):
                xw.append(k.op("vector", lambda E, dt_=dt_, e=e: E.tensor_scalar(
                    xlnc[:, dt_ * HC:(dt_ + 1) * HC],
                    xn1c[:, dt_ * HC:(dt_ + 1) * HC],
                    lngT[:, dt_ * 8 + e: dt_ * 8 + e + 1],
                    lnbT[:, dt_ * 8 + e: dt_ * 8 + e + 1],
                    OP.mult, OP.add),
                    waits=xl + [k.get("xlnc")], inc=1))
            for ht in range(FT):
                bank = pb[ht % 2]
                for kt in range(DT):
                    k.op("tensor", lambda E, ht=ht, kt=kt, e=e, bank=bank: E.matmul(
                        bank[:, 0:HC],
                        w1_sb[:, kt * FF + ht * 128: kt * FF + (ht + 1) * 128],
                        xlnc[:, kt * HC:(kt + 1) * HC],
                        start=(kt == 0), stop=(kt == DT - 1)),
                        waits=(w1l + xw + [k.get(("pbank", ht % 2))]) if kt == 0 else [],
                        inc=1 if kt == DT - 1 else 0)
                mmt = k.tok("tensor")
                th = k.op("scalar", lambda E, ht=ht, e=e, bank=bank: E.activation(
                    hm[:, ht * HC:(ht + 1) * HC], bank[:, 0:HC], AF.Relu,
                    bias=b1T[:, ht * 8 + e: ht * 8 + e + 1]),
                    waits=[mmt, k.get("hmfree")], inc=1)
                k.set(("pbank", ht % 2), th)
            h_done = k.tok("scalar")
            for m in range(DT):
                bank = pb[2 + m % 2]
                for k2 in range(FT):
                    k.op("tensor", lambda E, m=m, k2=k2, bank=bank: E.matmul(
                        bank[:, 0:HC],
                        w2_sb[:, k2 * D + m * 128: k2 * D + (m + 1) * 128],
                        hm[:, k2 * HC:(k2 + 1) * HC],
                        start=(k2 == 0), stop=(k2 == FT - 1)),
                        waits=(w2l + [h_done, k.get(("pbank", 2 + m % 2))])
                        if k2 == 0 else [],
                        inc=1 if k2 == FT - 1 else 0)
                mmt = k.tok("tensor")
                tev = k.op("vector", lambda E, m=m, msl=msl, bank=bank: E.scalar_tensor_tensor(
                    moeev[:, (m % 2) * HC:(m % 2 + 1) * HC], bank[:, 0:HC], 1.0,
                    mask_bc[:, msl * HC:(msl + 1) * HC], OP.mult, OP.mult),
                    waits=[mmt, tmb, k.get(("moeev", m % 2))], inc=1)
                k.set(("pbank", 2 + m % 2), tev)
                ts = k.dma("gpsimd", "stmoe%d" % (m % 2),
                           io["moe"][m, :, c2 * HC:(c2 + 1) * HC],
                           moeev[:, (m % 2) * HC:(m % 2 + 1) * HC],
                           waits=[tev, k.get(("moehbm", c2, m))],
                           **({} if e == 0 else {"accum_op": OP.add}))
                k.set(("moeev", m % 2), ts)
                k.set(("moehbm", c2, m), ts)
            k.set(("mask_bc", msl), k.tok("vector"))
            k.set("xn1c", k.tok("vector"))
            k.set("xlnc", k.tok("tensor"))
            k.set("hmfree", k.tok("tensor"))
        for kt in range(DT):
            k.set(("w1sb", kt), k.tok("tensor"))
        for k2 in range(FT):
            k.set(("w2sb", k2), k.tok("tensor"))
    moe_done = [k.dtok("stmoe0"), k.dtok("stmoe1"),
                k.tok("tensor"), k.tok("vector"), k.tok("scalar")]

    k.set("msk8w", moe_done)
    k.set("xres8w", moe_done)
    for c in range(NCHUNK):
        tm8 = k.dma("sync", "ldm8", msk8[:, :],
                    io["maskT"][:, c * TCH:(c + 1) * TCH],
                    waits=[k.get("msk8w")])
        tab = k.dma("gpsimd", "ldab", al_bc[:, :],
                    bass.AP(io["alpha"], c * TCH, [[0, 128], [1, TCH]]),
                    waits=[k.get("al_bc")])
        xrl = []
        for tt in range(4):
            xrl.append(k.dma(
                "gpsimd", "ldxr", xres8[:, tt * D:(tt + 1) * D],
                io["x1"][c * TCH + tt * 128: c * TCH + (tt + 1) * 128, :],
                waits=[k.get("xres8w")]))
        for dt_ in range(DT):
            sl = dt_ % 2
            tml = k.dma("sync", "ldmo%d" % sl, moec[:, sl * TCH:(sl + 1) * TCH],
                        io["moe"][dt_, :, c * TCH:(c + 1) * TCH],
                        waits=[k.get(("moec", sl)), moe_done if (c == 0 and dt_ < 2) else None])
            txl = k.dma("sync", "ldx18%d" % sl, xn1P8[:, sl * TCH:(sl + 1) * TCH],
                        io["xn1T"][dt_, :, c * TCH:(c + 1) * TCH],
                        waits=[k.get(("xn1P8", sl))])
            if c == 0:
                tzs = k.op("gpsimd", lambda E, sl=sl: E.memset(
                    s1c[:, sl * TCH: sl * TCH + 63], 0.0),
                    waits=[k.get(("s1c", sl))], inc=1)
                tsl = k.dma("sync", "lds1%d" % sl, s1c[:, sl * TCH + 63:(sl + 1) * TCH],
                            io["s1T"][dt_, :, 0:TCH - 63], waits=[tzs])
            else:
                tsl = k.dma("sync", "lds1%d" % sl, s1c[:, sl * TCH:(sl + 1) * TCH],
                            io["s1T"][dt_, :, c * TCH - 63:(c + 1) * TCH - 63],
                            waits=[k.get(("s1c", sl))])
            bank = pb[sl]
            tbm = k.op("tensor", lambda E, dt_=dt_, bank=bank: E.matmul(
                bank[:, :], b2_sb[:, dt_ * 128:(dt_ + 1) * 128], msk8[:, :],
                start=True, stop=True),
                waits=[tm8, tb2, k.get(("pbank", sl))], inc=1)
            t0 = k.op("vector", lambda E, sl=sl, bank=bank: E.tensor_add(
                g1c[:, sl * TCH:(sl + 1) * TCH], moec[:, sl * TCH:(sl + 1) * TCH],
                bank[:, :]),
                waits=[tbm, tml, k.get(("g1c", sl))], inc=1)
            k.set(("pbank", sl), t0)
            k.set(("moec", sl), t0)
            t1 = k.op("vector", lambda E, sl=sl: E.tensor_mul(
                g1c[:, sl * TCH:(sl + 1) * TCH], g1c[:, sl * TCH:(sl + 1) * TCH],
                al_bc[:, :]), waits=[t0, tab], inc=1)
            t2 = k.op("vector", lambda E, sl=sl: E.tensor_add(
                g1c[:, sl * TCH:(sl + 1) * TCH], g1c[:, sl * TCH:(sl + 1) * TCH],
                xn1P8[:, sl * TCH:(sl + 1) * TCH]), waits=[t1, txl], inc=1)
            k.set(("xn1P8", sl), t2)
            t3 = k.op("scalar", lambda E, sl=sl: E.activation(
                g1c[:, sl * TCH:(sl + 1) * TCH], g1c[:, sl * TCH:(sl + 1) * TCH],
                AF.Silu), waits=[t2], inc=1)
            t4 = k.op("vector", lambda E, sl=sl: E.tensor_mul(
                g1c[:, sl * TCH:(sl + 1) * TCH], s1c[:, sl * TCH:(sl + 1) * TCH],
                g1c[:, sl * TCH:(sl + 1) * TCH]), waits=[t3, tsl], inc=1)
            k.set(("s1c", sl), t4)
            for tt in range(4):
                ps = pbt[tt % 2][:, :]
                tp = k.op("tensor", lambda E, tt=tt, sl=sl, ps=ps: E.transpose(
                    ps, g1c[:, sl * TCH + tt * 128: sl * TCH + (tt + 1) * 128],
                    ident_f[:, :]),
                    waits=[t4, k.get(("pbtf", tt % 2))] + (xrl if dt_ == 0 else []),
                    inc=1)
                ta = k.op("vector", lambda E, tt=tt, dt_=dt_, ps=ps: E.tensor_add(
                    xres8[:, tt * D + dt_ * 128: tt * D + (dt_ + 1) * 128],
                    xres8[:, tt * D + dt_ * 128: tt * D + (dt_ + 1) * 128], ps),
                    waits=[tp], inc=1)
                k.set(("pbtf", tt % 2), ta)
            k.set(("g1c", sl), k.tok("tensor"))
        res_done = k.tok("vector")
        stt = []
        for tt in range(4):
            stt.append(k.dma(
                "gpsimd", "stout", io["out"][c * TCH + tt * 128: c * TCH + (tt + 1) * 128, :],
                xres8[:, tt * D:(tt + 1) * D], waits=[res_done]))
        k.set("xres8w", stt[-1])
        k.set("msk8w", k.tok("tensor"))
        k.set("al_bc", k.tok("vector"))
    return k.dtok("stout")


def build_l1b_program():
    nc = bass.Bass("TRN2", target_bir_lowering=False)
    io = {
        "x1": _dram(nc, "x1", (T, D), F32, "ExternalInput"),
        "xn1T": _dram(nc, "xn1T", (DT, 128, T), BF16, "ExternalInput"),
        "aff": _dram(nc, "aff", (T, 8), F32, "ExternalInput"),
        "s1T": _dram(nc, "s1T", (DT, 128, T), F32, "ExternalInput"),
        "ln_g": _dram(nc, "ln_g", (8, D), F32, "ExternalInput"),
        "ln_b": _dram(nc, "ln_b", (8, D), F32, "ExternalInput"),
        "w1": _dram(nc, "w1", (8, D, FF), F32, "ExternalInput"),
        "b1": _dram(nc, "b1", (8, FF), F32, "ExternalInput"),
        "w2": _dram(nc, "w2", (8, FF, D), F32, "ExternalInput"),
        "b2": _dram(nc, "b2", (8, D), F32, "ExternalInput"),
        "maskT": _dram(nc, "maskT", (8, T), BF16, "Internal"),
        "alpha": _dram(nc, "alpha", (T,), F32, "Internal"),
        "moe": _dram(nc, "moe", (DT, 128, T), F32, "Internal"),
        "out": _dram(nc, "out", (T, D), F32, "ExternalOutput"),
    }
    k = K(nc)
    k.pb = [nc.alloc_psum_tensor("pb%d" % i, [128, 512], F32) for i in range(4)]
    k.pbt = [nc.alloc_psum_tensor("pbt%d" % i, [128, 128], F32) for i in range(2)]
    k.pbb = [nc.alloc_psum_tensor("pbb%d" % i, [128, 128], BF16) for i in range(2)]
    emit_identities(k)
    last = build_l1b(k, io)
    k.op("sync", lambda E: E.nop(), waits=[last])
    k.emit()
    return nc


def l1b_in_map(inp, x1_core, xn1T_core, aff_core, s1T_core):
    return {
        "x1": np.ascontiguousarray(x1_core),
        "xn1T": np.ascontiguousarray(xn1T_core),
        "aff": np.ascontiguousarray(aff_core),
        "s1T": np.ascontiguousarray(s1T_core),
        "ln_g": inp["L1_ffn_ln_g"], "ln_b": inp["L1_ffn_ln_b"],
        "w1": inp["L1_ffn_w1"], "b1": inp["L1_ffn_b1"],
        "w2": inp["L1_ffn_w2"], "b2": inp["L1_ffn_b2"],
    }



def build_fused_program():
    nc = bass.Bass("TRN2", target_bir_lowering=False)
    mk = lambda n, s, d, kind: _dram(nc, n, s, d, kind)
    EI, EO, IN = "ExternalInput", "ExternalOutput", "Internal"
    io0 = {
        "x": mk("x", (T, D), F32, EI),
        "Wa": mk("Wa", (D, D), F32, EI), "ba": mk("ba", (D,), F32, EI),
        "Wb": mk("Wb", (D, D), F32, EI), "bb": mk("bb", (D,), F32, EI),
        "Wc": mk("Wc", (D, D), F32, EI), "bc": mk("bc", (D,), F32, EI),
        "w1": mk("l0w1", (FF, D), F32, EI), "b1": mk("l0b1", (FF,), F32, EI),
        "w2": mk("l0w2", (D, FF), F32, EI), "b2": mk("l0b2", (D,), F32, EI),
        "xnT": mk("xnT", (DT, 128, T), BF16, IN),
        "bw1T": mk("bw1T", (FT, 128, DT * 128), BF16, IN),
        "bw2T": mk("bw2T", (DT, 128, FT * 128), BF16, IN),
        "x1": mk("x1", (T, D), F32, IN),
    }
    io1 = {
        "x1": io0["x1"],
        "g_s": mk("g_s", (D,), F32, EI), "g_r": mk("g_r", (D,), F32, EI),
        "w_kv": mk("w_kv", (D, 2048), F32, EI),
        "w_q": mk("w_q", (D, D), F32, EI),
        "w_step": mk("w_step", (D, 8), F32, EI),
        "w_mom": mk("w_mom", (D, 8), F32, EI),
        "w_decay": mk("w_decay", (D, 8), F32, EI),
        "w_gate": mk("w_gate", (D, 8), F32, EI),
        "w_combine": mk("w_combine", (D, D), F32, EI),
        "mh_gamma": mk("mh_gamma", (8, 1, 128), F32, EI),
        "mem_w0": mk("mem_w0", (128, 128), F32, EI),
        "mem_w1": mk("mem_w1", (128, 128), F32, EI),
        "centroids": mk("centroids", (8, D), F32, EI),
        "rT": mk("rT", (DT, 128, T), BF16, IN),
        "kvT": mk("kvT", (16, 128, T), BF16, IN),
        "qT": mk("qT", (8, 128, T), BF16, IN),
        "lrq": mk("lrq", (8, T), F32, IN),
        "gateq": mk("gateq", (8, T), F32, IN),
        "md0": mk("md0", (16, 64), F32, IN),
        "grow": mk("grow", (16, 8192), F32, IN),
        "scrow": mk("scrow", (1, T), F32, IN),
        "valsT": mk("valsT", (8, 128, T), BF16, IN),
        "s1T": mk("s1T", (DT, 128, T), F32, IN),
        "xn1T": mk("xn1T", (DT, 128, T), BF16, IN),
        "aff": mk("aff", (T, 8), F32, IN),
        "xn1_tm": mk("xn1_tm", (T, D), BF16, IN),
    }
    io2 = {
        "x1": io0["x1"], "xn1T": io1["xn1T"], "aff": io1["aff"],
        "s1T": io1["s1T"], "xn1_tm": io1["xn1_tm"],
        "ln_g": mk("ln_g", (8, D), F32, EI), "ln_b": mk("ln_b", (8, D), F32, EI),
        "w1": mk("m_w1", (8, D, FF), F32, EI), "b1": mk("m_b1", (8, FF), F32, EI),
        "w2": mk("m_w2", (8, FF, D), F32, EI), "b2": mk("m_b2", (8, D), F32, EI),
        "out": mk("out", (T, D), F32, EO),
    }
    _sparse_scratch_io(nc, io2)
    k = K(nc)
    k.pb = [nc.alloc_psum_tensor("pb%d" % i, [128, 512], F32) for i in range(4)]
    k.pbt = [nc.alloc_psum_tensor("pbt%d" % i, [128, 128], F32) for i in range(2)]
    k.pbb = [nc.alloc_psum_tensor("pbb%d" % i, [128, 128], BF16) for i in range(2)]
    emit_identities(k)
    k.arena_init((nc.sbuf_bytes_remaining - 64) & ~31)
    build_l0(k, io0)
    k.full_sync()
    k.arena_reset()
    p5_done = build_l1a(k, io1)
    p6a_done = build_l1a_p6a(k, io1, p5_done)
    heads_done = build_l1a_p6b(k, io1, p6a_done)
    build_l1a_p6c(k, io1, heads_done)
    k.full_sync()
    k.arena_reset()
    last = build_l1b_sparse(k, io2)
    k.op("sync", lambda E: E.nop(), waits=[last])
    k.emit()
    return nc


def fused_in_map(inputs, x_core):
    m = {
        "x": x_core,
        "Wa": inputs["L0_attn_Wa"], "ba": inputs["L0_attn_ba"],
        "Wb": inputs["L0_attn_Wb"], "bb": inputs["L0_attn_bb"],
        "Wc": inputs["L0_attn_Wc"], "bc": inputs["L0_attn_bc"],
        "l0w1": inputs["L0_ffn_w1"], "l0b1": inputs["L0_ffn_b1"],
        "l0w2": inputs["L0_ffn_w2"], "l0b2": inputs["L0_ffn_b2"],
        "g_s": inputs["L1_attn_store_norm_g"],
        "g_r": inputs["L1_attn_retrieve_norm_g"],
        "w_kv": inputs["L1_attn_w_kv"], "w_q": inputs["L1_attn_w_q"],
        "w_step": inputs["L1_attn_w_step"], "w_mom": inputs["L1_attn_w_mom"],
        "w_decay": inputs["L1_attn_w_decay"], "w_gate": inputs["L1_attn_w_gate"],
        "w_combine": inputs["L1_attn_w_combine"],
        "mh_gamma": inputs["L1_attn_mh_gamma"],
        "mem_w0": inputs["L1_attn_mem_w0"], "mem_w1": inputs["L1_attn_mem_w1"],
        "centroids": inputs["L1_ffn_centroids"],
        "ln_g": inputs["L1_ffn_ln_g"], "ln_b": inputs["L1_ffn_ln_b"],
        "m_w1": inputs["L1_ffn_w1"], "m_b1": inputs["L1_ffn_b1"],
        "m_w2": inputs["L1_ffn_w2"], "m_b2": inputs["L1_ffn_b2"],
    }
    return {k_: np.ascontiguousarray(v) for k_, v in m.items()}


def kernel_fused(**inputs):
    cores = list(range(NCORES))
    x = np.ascontiguousarray(np.asarray(inputs["L0_attn_inputs"]))
    nc = build_fused_program()
    res = run_bass_kernel_spmd(
        nc, [fused_in_map(inputs, x[c]) for c in cores], core_ids=cores)
    return np.stack([res.results[c]["out"] for c in cores], axis=0)


def kernel(**inputs):
    return kernel_fused(**inputs)



GTILE = 256
NST = (T + NE * (GTILE - 1)) // GTILE + 1
SCAP = NST * GTILE
NWDMA = 8 + 32 + 2


def build_l1b_sparse(k, io):
    nc = k.nc
    pb = k.pb
    pbt = k.pbt
    pbb = k.pbb
    ident_bf = k.bufs["ident_bf"]
    ident_f = k.bufs["ident_f"]
    from concourse.bass_core_rust import IndirectOffsetOnAxis

    w1_sb = k.buf("w1_sb", [128, DT * FF], BF16)
    w2_sb = k.buf("w2_sb", [128, FT * D], BF16)
    hm = k.buf("hm", [128, FT * GTILE], BF16)
    xsT = k.buf("xsT", [128, DT * GTILE], BF16)
    xs_tm = k.buf("xs_tm", [128, 4 * D], BF16)
    o2f = k.buf("o2f", [128, 2 * GTILE], F32)
    otm = k.buf("otm", [128, 2 * D], F32)
    b1cur = k.buf("b1cur", [128, FT], F32)
    b2cur = k.buf("b2cur", [128, 8], F32)
    lncur = k.buf("lncur", [128, 16], F32)
    xlnT = k.buf("xlnT", [128, DT * GTILE], BF16)
    desti = k.buf("desti", [128, 32], I32)
    destf = k.buf("destf", [128, 32], F32)
    teidi = k.buf("teidi", [1, NST], I32)
    lngT = k.buf("lngT", [128, DT * 8], F32)
    lnbT = k.buf("lnbT", [128, DT * 8], F32)
    lnbTb = k.buf("lnbTb", [128, DT * 8], BF16)
    b2cols = k.buf("b2cols", [128, 64], F32)
    g8 = k.buf("g8", [8, 4096], F32)
    afft = k.buf("afft", [128, 2 * 8], F32)
    mcol = k.buf("mcol", [128, 4], F32)
    mkb = k.buf("mkb", [128, 2 * 8], BF16)
    mrow = k.buf("mrow", [8, 2 * 128], BF16)
    ones8b = k.buf("ones8b", [8, 1], BF16)
    ones8f = k.buf("ones8f", [8, 1], F32)
    rowsc = k.buf("rowsc", [1, 4 * 64], F32)
    rowi = k.buf("rowi", [1, 64], I32)
    ge8 = k.buf("ge8", [8, NST], F32)
    ge8b = k.buf("ge8b", [8, NST], BF16)
    iotab = k.buf("iotab", [8, NST], F32)
    drow = k.buf("drow", [1, 2 * TCH], F32)
    ranks = k.overlay("ranks", w1_sb, [8, T], F32, byte_off=0)
    mskb2 = k.overlay("mskb2", w1_sb, [8, T], BF16, byte_off=16384)
    w1bf = k.overlay("w1bf", w2_sb, [128, DT * FF], BF16, byte_off=0)
    w1st = k.overlay("w1st", w1_sb, [128, FF], BF16, byte_off=32768)
    b1rowq = k.overlay("b1rowq", w1_sb, [1, FF], F32, byte_off=49152)
    moeTc = k.overlay("moeTc", w1_sb, [128, DT * TCH], F32, byte_off=16384)
    mo_tm = k.overlay("mo_tm", w1_sb, [128, 2 * D], F32, byte_off=0)
    al_bc = k.overlay("al_bc", w2_sb, [128, TCH], F32, byte_off=0)
    g1c = k.overlay("g1c", w2_sb, [128, 2 * TCH], F32, byte_off=2048)
    s1c = k.overlay("s1c", w2_sb, [128, 2 * TCH], F32, byte_off=6144)
    xn1P8 = k.overlay("xn1P8", w2_sb, [128, 2 * TCH], BF16, byte_off=10240)
    xres8 = k.overlay("xres8", w2_sb, [128, 4 * D], F32, byte_off=12288)

    kz0 = k.op("gpsimd", lambda E: E.memset(ones8b[:, :], 1.0), inc=1)
    kz = k.op("gpsimd", lambda E: E.memset(ones8f[:, :], 1.0), inc=1)

    tg8 = k.dma("sync", "ldg8", g8[:, 0:1024], io["ln_g"][:, :])
    tb8 = k.dma("sync", "ldg8", g8[:, 1024:2048], io["ln_b"][:, :])
    tb2r = k.dma("sync", "ldg8", g8[:, 2048:3072], io["b2"][:, :])
    for dt_ in range(DT):
        for which, dst, boff in (("g", lngT, 0), ("b", lnbT, 1024)):
            tp = k.op("tensor", lambda E, dt_=dt_, boff=boff: E.transpose(
                pbt[dt_ % 2][:, 0:8],
                g8[:, boff + dt_ * 128: boff + (dt_ + 1) * 128],
                ident_f[0:8, 0:8]),
                waits=[tg8, tb8, tb2r, k.get(("pbtf", dt_ % 2)), k.get("ident")],
                inc=1)
            te = k.op("vector", lambda E, dt_=dt_, dst=dst: E.tensor_copy(
                dst[:, dt_ * 8:(dt_ + 1) * 8], pbt[dt_ % 2][:, 0:8]),
                waits=[tp], inc=1)
            k.set(("pbtf", dt_ % 2), te)
        tp = k.op("tensor", lambda E, dt_=dt_: E.transpose(
            pbt[dt_ % 2][:, 0:8], g8[:, 2048 + dt_ * 128: 2048 + (dt_ + 1) * 128],
            ident_f[0:8, 0:8]),
            waits=[k.get(("pbtf", dt_ % 2))], inc=1)
        te = k.op("vector", lambda E, dt_=dt_: E.tensor_copy(
            bass.AP(b2cols, dt_, [[64, 128], [8, 8]]), pbt[dt_ % 2][:, 0:8]),
            waits=[tp], inc=1)
        k.set(("pbtf", dt_ % 2), te)
    tlb = k.op("vector", lambda E: E.tensor_copy(lnbTb[:, :], lnbT[:, :]),
               waits=[k.tok("vector")], inc=1)
    for e in range(NE):
        k.dma("gpsimd", "stb2p", io["b2pc"][e, :, :],
              bass.AP(b2cols, e * 8, [[64, 128], [1, 8]]),
              waits=[k.tok("vector")], allow_slow_non_contiguous=True)
    prep_cols_v = k.tok("vector")

    b1T_sb = k.overlay("b1T_sb", w1_sb, [128, FT * 8], F32, byte_off=40960)
    tb1a = k.dma("sync", "ldb1a", g8[:, :], io["b1"][:, :],
                 waits=[k.tok("tensor")])
    for ht in range(FT):
        tp = k.op("tensor", lambda E, ht=ht: E.transpose(
            pbt[ht % 2][:, 0:8], g8[:, ht * 128:(ht + 1) * 128],
            ident_f[0:8, 0:8]),
            waits=[tb1a, k.get(("pbtf", ht % 2))], inc=1)
        te = k.op("vector", lambda E, ht=ht: E.tensor_copy(
            b1T_sb[:, ht * 8:(ht + 1) * 8], pbt[ht % 2][:, 0:8]),
            waits=[tp], inc=1)
        k.set(("pbtf", ht % 2), te)
    tb1v = k.tok("vector")
    for e in range(NE):
        k.dma("gpsimd", "stb1pc", io["b1pc"][e, :, :],
              bass.AP(b1T_sb, e, [[256, 128], [8, 32]]),
              waits=[tb1v], allow_slow_non_contiguous=True)
        k.dma("gpsimd", "stlnpc", io["lnpc"][e, :, 0:8],
              bass.AP(lngT, e, [[64, 128], [8, 8]]),
              waits=[prep_cols_v], allow_slow_non_contiguous=True)
        k.dma("gpsimd", "stlnpc", io["lnpc"][e, :, 8:16],
              bass.AP(lnbT, e, [[64, 128], [8, 8]]),
              waits=[prep_cols_v], allow_slow_non_contiguous=True)
    wprep_done = [k.dtok("stb1pc"), k.dtok("stlnpc"), k.dtok("stb2p")]

    for tt in range(T // 128):
        sl = tt % 2
        tl = k.dma("sync", "ldaff%d" % sl, afft[:, sl * 8:(sl + 1) * 8],
                   io["aff"][tt * 128:(tt + 1) * 128, :],
                   waits=[k.get(("afft", sl))])
        tmx = k.op("vector", lambda E, sl=sl: E.reduce_max(
            mcol[:, 0:1], afft[:, sl * 8:(sl + 1) * 8], AX),
            waits=[tl, k.get("mcol")], inc=1)
        tal = k.op("scalar", lambda E: E.activation(
            mcol[:, 1:2], mcol[:, 0:1], AF.Sigmoid), waits=[tmx], inc=1)
        tsa = k.dma("gpsimd", "stal",
                    io["alpha"][tt * 128:(tt + 1) * 128].rearrange(
                        "(p o) -> p o", o=1),
                    mcol[:, 1:2], waits=[tal])
        k.set("mcol", tsa)
        tmk = k.op("vector", lambda E, sl=sl: E.tensor_scalar(
            mkb[:, sl * 8:(sl + 1) * 8], afft[:, sl * 8:(sl + 1) * 8],
            mcol[:, 0:1], None, OP.is_equal),
            waits=[tmx, k.get(("mkb", sl))], inc=1)
        k.set(("afft", sl), tmk)
        tp = k.op("tensor", lambda E, sl=sl: E.transpose(
            pbb[sl][0:8, :], mkb[:, sl * 8:(sl + 1) * 8], ident_bf[:, :]),
            waits=[tmk, k.get(("pbf", sl)), k.get("ident")], inc=1)
        k.set(("mkb", sl), tp)
        tev = k.op("vector", lambda E, sl=sl, tt=tt: E.tensor_copy(
            mskb2[:, tt * 128:(tt + 1) * 128], pbb[sl][0:8, :]),
            waits=[tp, k.get("mskb2w")], inc=1)
        k.set(("pbf", sl), tev)
    masks_v = k.tok("vector")

    onesr = k.overlay("onesr", w1_sb, [8, T], BF16, byte_off=24576)
    tz1 = k.op("gpsimd", lambda E: E.memset(onesr[:, :], 1.0),
               waits=[masks_v], inc=1)
    trk = k.op("vector", lambda E: E.tensor_tensor_scan(
        ranks[:, :], onesr[:, :], mskb2[:, :], -1.0, OP.mult, OP.add),
        waits=[masks_v, tz1], inc=1)
    tcc = k.op("vector", lambda E: E.tensor_scalar_add(
        mcol[0:8, 2:3], ranks[:, T - 1:T], 1.0), waits=[trk, k.get("mcol")], inc=1)
    tpc = k.op("tensor", lambda E: E.transpose(
        pbt[0][0:1, 0:8], mcol[0:8, 2:3], ident_f[0:8, 0:8]),
        waits=[tcc, k.get(("pbtf", 0))], inc=1)
    tcr = k.op("vector", lambda E: E.tensor_copy(
        rowsc[0:1, 0:8], pbt[0][0:1, 0:8]), waits=[tpc], inc=1)
    k.set(("pbtf", 0), tcr)
    t1 = k.op("vector", lambda E: E.tensor_copy(rowi[0:1, 0:8], rowsc[0:1, 0:8]),
              waits=[tcr], inc=1)
    t2 = k.op("vector", lambda E: E.tensor_scalar(
        rowi[0:1, 8:16], rowi[0:1, 0:8], GTILE - 1, None, OP.add),
        waits=[t1], inc=1)
    t3 = k.op("vector", lambda E: E.tensor_scalar(
        rowi[0:1, 8:16], rowi[0:1, 8:16], -GTILE, None, OP.bitwise_and),
        waits=[t2], inc=1)
    t4 = k.op("vector", lambda E: E.tensor_copy(rowsc[0:1, 8:16], rowi[0:1, 8:16]),
              waits=[t3], inc=1)
    t4b = k.op("vector", lambda E: E.tensor_scalar(
        rowi[0:1, 16:24], rowi[0:1, 8:16], 8, None, OP.logical_shift_right),
        waits=[t3], inc=1)
    tone = k.op("gpsimd", lambda E: E.memset(rowsc[0:1, 16:24], 1.0), inc=1)
    t5 = k.op("vector", lambda E: E.tensor_tensor_scan(
        rowsc[0:1, 24:32], rowsc[0:1, 16:24], rowsc[0:1, 8:16], 0.0,
        OP.mult, OP.add), waits=[t4, tone], inc=1)
    t6 = k.op("vector", lambda E: E.tensor_sub(
        rowsc[0:1, 32:40], rowsc[0:1, 24:32], rowsc[0:1, 8:16]),
        waits=[t5], inc=1)
    tsr = k.dma("gpsimd", "strow0", io["rrows"][0, 0:8].rearrange("(a b) -> a b", a=1),
                rowsc[0:1, 24:32], waits=[t6])
    tsb2 = k.dma("gpsimd", "strow1", io["rrows"][1, 0:8].rearrange("(a b) -> a b", a=1),
                 rowsc[0:1, 32:40], waits=[t6])
    tbc = k.dma("sync", "ldrow0", mcol[0:8, 3:4],
                io["rrows"][0, 0:8].rearrange("(p o) -> p o", o=1),
                waits=[tsr, k.get("mcol3")])
    tz_a = k.op("gpsimd", lambda E: E.memset(rowsc[0:1, 64:64 + NST], float(GTILE)),
                inc=1)
    tz_b = k.op("gpsimd", lambda E: E.memset(rowsc[0:1, 96:96 + NST], 1.0), inc=1)
    tio = k.op("vector", lambda E: E.tensor_tensor_scan(
        rowsc[0:1, 40:40 + NST], rowsc[0:1, 96:96 + NST],
        rowsc[0:1, 64:64 + NST], -float(GTILE), OP.mult, OP.add),
        waits=[tz_a, tz_b], inc=1)
    tis = k.dma("gpsimd", "strow2", io["rrows"][2, 0:NST].rearrange("(a b) -> a b", a=1),
                rowsc[0:1, 40:40 + NST], waits=[tio])
    tib = k.dma("sync", "ldrow2", iotab[:, :],
                bass.AP(io["rrows"], 2 * 64, [[0, 8], [1, NST]]),
                waits=[tis])
    tge = k.op("vector", lambda E: E.tensor_scalar(
        ge8[:, :], iotab[:, :], mcol[0:8, 3:4], None, OP.is_ge),
        waits=[tib, tbc], inc=1)
    tgeb = k.op("vector", lambda E: E.tensor_copy(ge8b[:, :], ge8[:, :]),
                waits=[tge], inc=1)
    tmm = k.op("tensor", lambda E: E.matmul(
        pbt[1][0:1, 0:NST], ones8b[:, :], ge8b[:, :], start=True, stop=True),
        waits=[tgeb, kz0, k.get(("pbtf", 1))], inc=1)
    tev = k.op("vector", lambda E: E.tensor_scalar_min(
        rowsc[0:1, 40:40 + NST], pbt[1][0:1, 0:NST], float(NE - 1)),
        waits=[tmm], inc=1)
    k.set(("pbtf", 1), tev)
    tti = k.op("vector", lambda E: E.tensor_copy(
        teidi[0:1, :], rowsc[0:1, 40:40 + NST]), waits=[tev], inc=1)
    tbcl = k.dma("sync", "ldrow1", mcol[0:8, 2:3],
                 io["rrows"][1, 0:8].rearrange("(p o) -> p o", o=1),
                 waits=[tsb2])
    r2 = k.overlay("r2", w1_sb, [8, T], F32, byte_off=0)
    tr2 = k.op("vector", lambda E: E.tensor_scalar_add(
        r2[:, :], ranks[:, :], mcol[0:8, 2:3]), waits=[tbcl, trk], inc=1)
    tm2 = k.op("vector", lambda E: E.tensor_mul(
        r2[:, :], r2[:, :], mskb2[:, :]), waits=[tr2], inc=1)
    for cq in range(NCHUNK):
        tdm = k.op("tensor", lambda E, cq=cq: E.matmul(
            pb[3][0:1, :], ones8f[:, :], r2[:, cq * TCH:(cq + 1) * TCH],
            start=True, stop=True),
            waits=[tm2, kz, k.get(("pbank", 3))], inc=1)
        tde = k.op("scalar", lambda E, cq=cq: E.copy(
            drow[0:1, (cq % 2) * TCH:(cq % 2 + 1) * TCH], pb[3][0:1, :]),
            waits=[tdm, k.get(("drow", cq % 2))], inc=1)
        k.set(("pbank", 3), tde)
        tds = k.dma("gpsimd", "stdest%d" % (cq % 2),
                    io["dest_q"][cq * TCH:(cq + 1) * TCH].rearrange(
                        "(a b) -> a b", a=1),
                    drow[0:1, (cq % 2) * TCH:(cq % 2 + 1) * TCH], waits=[tde])
        k.set(("drow", cq % 2), tds)
    dcl = []
    for tt in range(32):
        dcl.append(k.dma("sync", "lddc", destf[:, tt:tt + 1],
                         io["dest_q"][tt * 128:(tt + 1) * 128].rearrange(
                             "(p o) -> p o", o=1),
                         waits=[k.dtok("stdest0"), k.dtok("stdest1")]))
    tdi = k.op("vector", lambda E: E.tensor_copy(desti[:, :], destf[:, :]),
               waits=dcl, inc=1)

    tzx = k.op("gpsimd", lambda E: E.memset(xs_tm[:, :], 0.0),
               waits=[k.get(("xs_tm", 0)), k.get(("xs_tm", 1))], inc=1)
    zf = []
    for st in range(2 * NST):
        zf.append(k.dma("sync", "zfs%d" % (st % 2),
                        io["sorted_x"][st * 128:(st + 1) * 128, :],
                        xs_tm[:, (st % 2) * D:(st % 2 + 1) * D], waits=[tzx]))
    k.set(("xs_tm", 0), k.dtok("zfs0"))
    k.set(("xs_tm", 1), k.dtok("zfs1"))
    k.set(("xs_tm", 2), tzx)
    k.set(("xs_tm", 3), tzx)
    for tt in range(32):
        sl = tt % 4
        txl = k.dma("sync", "ldxtm%d" % sl, xs_tm[:, sl * D:(sl + 1) * D],
                    io["xn1_tm"][tt * 128:(tt + 1) * 128, :],
                    waits=[k.get(("xs_tm", sl))])
        tsc = k.op("gpsimd", lambda E, tt=tt, sl=sl: E.indirect_dma_start(
            io["sorted_x"][:, :], IndirectOffsetOnAxis(desti[:, tt:tt + 1], 0),
            xs_tm[:, sl * D:(sl + 1) * D], None),
            waits=[txl, tdi, k.dtok("zfs0"), k.dtok("zfs1")],
            inc=("sem", "D_scat%d" % sl, 16))
        k.set(("xs_tm", sl), tsc)
    scat_done = [k.dtok("scat%d" % i) for i in range(4)]

    if "D_swl" not in k.sems:
        k.sems["D_swl"] = nc.alloc_semaphore(name="D_swl")
        k.cnt["D_swl"] = 0
    if "S_tdone" not in k.sems:
        k.sems["S_tdone"] = nc.alloc_semaphore(name="S_tdone")
        k.cnt["S_tdone"] = 0
    NWD = 5

    def weight_loop(E):
        with E.register("rc") as rc, E.register("rtot") as rtot, \
             E.register("rld") as rld, E.register("rph") as rph:
            E.reg_mov(rtot, 0)
            E.reg_mov(rld, 0)
            for e in range(NE):
                E.reg_load(rc, rowi[0:1, 16 + e:17 + e])
                with E.If_ne(rc, 0):
                    E.wait_ge(k.sems["S_tdh"], rtot)
                    E.dma_start(
                        bass.AP(w1_sb, 0, [[32768, 128], [FF, DT], [1, FF]]),
                        bass.AP(io["w1"], e * D * FF,
                                [[FF, 128], [128 * FF, DT], [1, FF]]),
                    ).then_inc(k.sems["D_ldw"], 16)
                    E.wait_ge(k.sems["S_tdone"], rtot)
                    E.dma_start(
                        bass.AP(w2_sb, 0, [[32768, 128], [D, FT], [1, D]]),
                        bass.AP(io["w2"], e * FF * D,
                                [[D, 128], [128 * D, FT], [1, D]]),
                    ).then_inc(k.sems["D_ldw"], 16)
                    E.dma_start(b1cur[:, :],
                                bass.AP(io["b1pc"], e * 128 * 32,
                                        [[32, 128], [1, 32]])
                                ).then_inc(k.sems["D_ldw"], 16)
                    E.dma_start(b2cur[:, :],
                                bass.AP(io["b2pc"], e * 128 * 8,
                                        [[8, 128], [1, 8]])
                                ).then_inc(k.sems["D_ldw"], 16)
                    E.dma_start(lncur[:, :],
                                bass.AP(io["lnpc"], e * 128 * 16,
                                        [[16, 128], [1, 16]])
                                ).then_inc(k.sems["D_ldw"], 16)
                    E.reg_add(rld, rld, 16 * NWD)
                    E.wait_ge(k.sems["D_ldw"], rld)
                with E.Fori(0, NST) as j:
                    with E.If_ne(rc, 0):
                        E.sem_inc(k.sems["D_swl"], 16 * NWD)
                        E.reg_sub(rc, rc, 1)
                        E.reg_add(rtot, rtot, 1)
            E.reg_mov(rph, NST)
            E.reg_sub(rph, rph, rtot)
            with E.Fori(0, NST) as j:
                with E.If_ne(rph, 0):
                    E.sem_inc(k.sems["D_swl"], 16 * NWD)
                    E.reg_sub(rph, rph, 1)

    if "D_ldw" not in k.sems:
        k.sems["D_ldw"] = nc.alloc_semaphore(name="D_ldw")
        k.cnt["D_ldw"] = 0
    if "S_tdh" not in k.sems:
        k.sems["S_tdh"] = nc.alloc_semaphore(name="S_tdh")
        k.cnt["S_tdh"] = 0
    k.op("gpsimd", lambda E: weight_loop(E),
         waits=[k.tok("vector"), tdi, t4b] + wprep_done, inc=0)
    k.cnt["D_swl"] += 16 * NWD * NST

    for st in range(NST):
        swl_tok = ("D_swl", 16 * NWD * (st + 1))
        sl_ = st % 2
        stoks = []
        for sub in range(2):
            stoks.append(k.dma(
                "sync", "ldsx%d" % sub,
                xs_tm[:, sub * D:(sub + 1) * D],
                io["sorted_x"][st * GTILE + sub * 128: st * GTILE + (sub + 1) * 128, :],
                waits=scat_done + [k.get(("xs_tm2", sub))]))
        tps = []
        for sub in range(2):
            for dt_ in range(DT):
                ps = pbb[dt_ % 2][:, :]
                tp = k.op("tensor", lambda E, sub=sub, dt_=dt_, ps=ps: E.transpose(
                    ps, xs_tm[:, sub * D + dt_ * 128: sub * D + (dt_ + 1) * 128],
                    ident_bf[:, :]),
                    waits=[stoks[sub], k.get(("pbf", dt_ % 2))], inc=1)
                tev = k.op("vector", lambda E, sub=sub, dt_=dt_, ps=ps: E.tensor_copy(
                    xsT[:, dt_ * GTILE + sub * 128: dt_ * GTILE + (sub + 1) * 128],
                    ps),
                    waits=[tp, k.get("xsT_g")], inc=1)
                k.set(("pbf", dt_ % 2), tev)
                tps.append(tev)
        k.set(("xs_tm2", 0), k.tok("tensor"))
        k.set(("xs_tm2", 1), k.tok("tensor"))
        lnt = []
        for dt_ in range(DT):
            lnt.append(k.op("vector", lambda E, dt_=dt_: E.tensor_scalar(
                xlnT[:, dt_ * GTILE:(dt_ + 1) * GTILE],
                xsT[:, dt_ * GTILE:(dt_ + 1) * GTILE],
                lncur[:, dt_:dt_ + 1], lncur[:, 8 + dt_:9 + dt_],
                OP.mult, OP.add),
                waits=[k.tok("vector"), swl_tok, k.get("xlnT_g")], inc=1))
        for ht in range(FT):
            bank = pb[ht % 2]
            for kt in range(DT):
                k.op("tensor", lambda E, ht=ht, kt=kt, bank=bank: E.matmul(
                    bank[:, 0:GTILE],
                    w1_sb[:, kt * FF + ht * 128: kt * FF + (ht + 1) * 128],
                    xlnT[:, kt * GTILE:(kt + 1) * GTILE],
                    start=(kt == 0), stop=(kt == DT - 1)),
                    waits=(lnt + [swl_tok, k.get(("pbank", ht % 2))])
                    if kt == 0 else [],
                    inc=1 if kt == DT - 1 else 0)
            mmt = k.tok("tensor")
            th = k.op("scalar", lambda E, ht=ht, bank=bank: E.activation(
                hm[:, ht * GTILE:(ht + 1) * GTILE], bank[:, 0:GTILE], AF.Relu,
                bias=b1cur[:, ht:ht + 1]),
                waits=[mmt, k.get("hmfree")], inc=1)
            k.set(("pbank", ht % 2), th)
        h_done = k.tok("scalar")
        k.op("vector", lambda E: E.engine_nop(), waits=[k.tok("tensor")],
             inc=("sem", "S_tdh", 1))
        for m in range(DT):
            bank = pb[2 + m % 2]
            for k2 in range(FT):
                k.op("tensor", lambda E, m=m, k2=k2, bank=bank: E.matmul(
                    bank[:, 0:GTILE],
                    w2_sb[:, k2 * D + m * 128: k2 * D + (m + 1) * 128],
                    hm[:, k2 * GTILE:(k2 + 1) * GTILE],
                    start=(k2 == 0), stop=(k2 == FT - 1)),
                    waits=[h_done, swl_tok, k.get(("pbank", 2 + m % 2))]
                    if k2 == 0 else [],
                    inc=1 if k2 == FT - 1 else 0)
            mmt = k.tok("tensor")
            tev = k.op("vector", lambda E, m=m, bank=bank: E.tensor_scalar_add(
                o2f[:, (m % 2) * GTILE:(m % 2 + 1) * GTILE], bank[:, 0:GTILE],
                b2cur[:, m:m + 1]),
                waits=[mmt, k.get(("o2f", m % 2))], inc=1)
            k.set(("pbank", 2 + m % 2), tev)
            for sub in range(2):
                ps = pbt[sub][:, :]
                tp = k.op("tensor", lambda E, m=m, sub=sub, ps=ps: E.transpose(
                    ps, o2f[:, (m % 2) * GTILE + sub * 128:
                            (m % 2) * GTILE + (sub + 1) * 128], ident_f[:, :]),
                    waits=[tev, k.get(("pbtf", sub))], inc=1)
                to = k.op("vector", lambda E, m=m, sub=sub, ps=ps: E.tensor_copy(
                    otm[:, sub * D + m * 128: sub * D + (m + 1) * 128], ps),
                    waits=[tp, k.get(("otm", sub, m))], inc=1)
                k.set(("pbtf", sub), to)
                k.set(("otm", sub, m), to)
            k.set(("o2f", m % 2), k.tok("tensor"))
        k.op("vector", lambda E: E.engine_nop(), waits=[k.tok("tensor")],
             inc=("sem", "S_tdone", 1))
        odone = k.tok("vector")
        for sub in range(2):
            ts = k.dma("sync", "stso%d" % sub,
                       io["sorted_moe"][st * GTILE + sub * 128:
                                        st * GTILE + (sub + 1) * 128, :],
                       otm[:, sub * D:(sub + 1) * D], waits=[odone])
            for m in range(DT):
                k.set(("otm", sub, m), ts)
        k.set("xsT_g", k.tok("vector"))
        k.set("xlnT_g", k.tok("tensor"))
        k.set("hmfree", k.tok("tensor"))
    tiles_done = [k.dtok("stso0"), k.dtok("stso1"),
                  k.tok("tensor"), k.tok("vector"), k.tok("scalar")]

    k.set("mo_gate", tiles_done)
    k.set("xres8w", tiles_done)
    k.set("al_gate", tiles_done)
    for c in range(NCHUNK):
        for q in range(4):
            sl = q % 2
            tg = k.op("gpsimd", lambda E, c=c, q=q, sl=sl: E.indirect_dma_start(
                mo_tm[:, sl * D:(sl + 1) * D], None,
                io["sorted_moe"][:, :],
                IndirectOffsetOnAxis(desti[:, c * 4 + q:c * 4 + q + 1], 0)),
                waits=[k.get(("mo_tm", sl)), k.get("mo_gate")],
                inc=("sem", "D_mog%d" % sl, 16))
            for dt_ in range(DT):
                ps = pbt[dt_ % 2][:, :]
                tp = k.op("tensor", lambda E, sl=sl, dt_=dt_, ps=ps: E.transpose(
                    ps, mo_tm[:, sl * D + dt_ * 128: sl * D + (dt_ + 1) * 128],
                    ident_f[:, :]),
                    waits=[tg, k.get(("pbtf", dt_ % 2))], inc=1)
                te = k.op("vector", lambda E, q=q, dt_=dt_, ps=ps: E.tensor_copy(
                    moeTc[:, dt_ * TCH + q * 128:dt_ * TCH + (q + 1) * 128], ps),
                    waits=[tp, k.get("moeTc_g")], inc=1)
                k.set(("pbtf", dt_ % 2), te)
            k.set(("mo_tm", sl), k.tok("tensor"))
        gat_v = k.tok("vector")
        tab = k.dma("gpsimd", "ldab", al_bc[:, :],
                    bass.AP(io["alpha"], c * TCH, [[0, 128], [1, TCH]]),
                    waits=[k.get("al_gate")])
        xrl = []
        for tt in range(4):
            xrl.append(k.dma(
                "gpsimd", "ldxr", xres8[:, tt * D:(tt + 1) * D],
                io["x1"][c * TCH + tt * 128: c * TCH + (tt + 1) * 128, :],
                waits=[k.get("xres8w")]))
        for dt_ in range(DT):
            sl = dt_ % 2
            txl = k.dma("sync", "ldx18%d" % sl, xn1P8[:, sl * TCH:(sl + 1) * TCH],
                        io["xn1T"][dt_, :, c * TCH:(c + 1) * TCH],
                        waits=[k.get(("xn1P8", sl))])
            if c == 0:
                tzs = k.op("gpsimd", lambda E, sl=sl: E.memset(
                    s1c[:, sl * TCH: sl * TCH + 63], 0.0),
                    waits=[k.get(("s1c", sl)), tiles_done], inc=1)
                tsl = k.dma("sync", "lds1%d" % sl, s1c[:, sl * TCH + 63:(sl + 1) * TCH],
                            io["s1T"][dt_, :, 0:TCH - 63], waits=[tzs])
            else:
                tsl = k.dma("sync", "lds1%d" % sl, s1c[:, sl * TCH:(sl + 1) * TCH],
                            io["s1T"][dt_, :, c * TCH - 63:(c + 1) * TCH - 63],
                            waits=[k.get(("s1c", sl))])
            t1 = k.op("vector", lambda E, sl=sl, dt_=dt_: E.tensor_mul(
                g1c[:, sl * TCH:(sl + 1) * TCH],
                moeTc[:, dt_ * TCH:(dt_ + 1) * TCH], al_bc[:, :]),
                waits=[gat_v, tab, k.get(("g1c", sl))], inc=1)
            t2 = k.op("vector", lambda E, sl=sl: E.tensor_add(
                g1c[:, sl * TCH:(sl + 1) * TCH], g1c[:, sl * TCH:(sl + 1) * TCH],
                xn1P8[:, sl * TCH:(sl + 1) * TCH]), waits=[t1, txl], inc=1)
            k.set(("xn1P8", sl), t2)
            t3 = k.op("scalar", lambda E, sl=sl: E.activation(
                g1c[:, sl * TCH:(sl + 1) * TCH], g1c[:, sl * TCH:(sl + 1) * TCH],
                AF.Silu), waits=[t2], inc=1)
            t4 = k.op("vector", lambda E, sl=sl: E.tensor_mul(
                g1c[:, sl * TCH:(sl + 1) * TCH], s1c[:, sl * TCH:(sl + 1) * TCH],
                g1c[:, sl * TCH:(sl + 1) * TCH]), waits=[t3, tsl], inc=1)
            k.set(("s1c", sl), t4)
            for tt in range(4):
                ps = pbt[tt % 2][:, :]
                tp = k.op("tensor", lambda E, tt=tt, sl=sl, ps=ps: E.transpose(
                    ps, g1c[:, sl * TCH + tt * 128: sl * TCH + (tt + 1) * 128],
                    ident_f[:, :]),
                    waits=[t4, k.get(("pbtf", tt % 2))] + (xrl if dt_ == 0 else []),
                    inc=1)
                ta = k.op("vector", lambda E, tt=tt, dt_=dt_, ps=ps: E.tensor_add(
                    xres8[:, tt * D + dt_ * 128: tt * D + (dt_ + 1) * 128],
                    xres8[:, tt * D + dt_ * 128: tt * D + (dt_ + 1) * 128], ps),
                    waits=[tp], inc=1)
                k.set(("pbtf", tt % 2), ta)
            k.set(("g1c", sl), k.tok("tensor"))
        res_done = k.tok("vector")
        stt = []
        for tt in range(4):
            stt.append(k.dma(
                "gpsimd", "stout",
                io["out"][c * TCH + tt * 128: c * TCH + (tt + 1) * 128, :],
                xres8[:, tt * D:(tt + 1) * D], waits=[res_done]))
        k.set("xres8w", stt[-1])
        k.set("moeTc_g", k.tok("vector"))
        k.set("mo_gate", [k.tok("vector")])
        k.set("al_gate", k.tok("vector"))
    return k.dtok("stout")


def _sparse_scratch_io(nc, io):
    io.update({
        "maskT": _dram(nc, "maskT", (8, T), BF16, "Internal"),
        "alpha": _dram(nc, "alpha_q", (T,), F32, "Internal"),
        "dest_q": _dram(nc, "dest_q", (T,), F32, "Internal"),
        "rrows": _dram(nc, "rrows", (3, 64), F32, "Internal"),
        "sorted_x": _dram(nc, "sorted_x", (SCAP, D), BF16, "Internal"),
        "sorted_moe": _dram(nc, "sorted_moe", (SCAP, D), F32, "Internal"),
        "b1pc": _dram(nc, "b1pc", (8, 128, 32), F32, "Internal"),
        "lnpc": _dram(nc, "lnpc", (8, 128, 16), F32, "Internal"),
        "b2pc": _dram(nc, "b2pc", (8, 128, 8), F32, "Internal"),
    })
    return io


def build_l1bs_program():
    nc = bass.Bass("TRN2", target_bir_lowering=False)
    io = {
        "x1": _dram(nc, "x1", (T, D), F32, "ExternalInput"),
        "xn1_tm": _dram(nc, "xn1_tm", (T, D), BF16, "ExternalInput"),
        "xn1T": _dram(nc, "xn1T", (DT, 128, T), BF16, "ExternalInput"),
        "aff": _dram(nc, "aff", (T, 8), F32, "ExternalInput"),
        "s1T": _dram(nc, "s1T", (DT, 128, T), F32, "ExternalInput"),
        "ln_g": _dram(nc, "ln_g", (8, D), F32, "ExternalInput"),
        "ln_b": _dram(nc, "ln_b", (8, D), F32, "ExternalInput"),
        "w1": _dram(nc, "w1", (8, D, FF), F32, "ExternalInput"),
        "b1": _dram(nc, "b1", (8, FF), F32, "ExternalInput"),
        "w2": _dram(nc, "w2", (8, FF, D), F32, "ExternalInput"),
        "b2": _dram(nc, "b2", (8, D), F32, "ExternalInput"),
        "out": _dram(nc, "out", (T, D), F32, "ExternalOutput"),
    }
    _sparse_scratch_io(nc, io)
    k = K(nc)
    k.pb = [nc.alloc_psum_tensor("pb%d" % i, [128, 512], F32) for i in range(4)]
    k.pbt = [nc.alloc_psum_tensor("pbt%d" % i, [128, 128], F32) for i in range(2)]
    k.pbb = [nc.alloc_psum_tensor("pbb%d" % i, [128, 128], BF16) for i in range(2)]
    emit_identities(k)
    k.arena_init((nc.sbuf_bytes_remaining - 64) & ~31)
    last = build_l1b_sparse(k, io)
    k.op("sync", lambda E: E.nop(), waits=[last])
    k.emit()
    return nc
